# version 22
# baseline (speedup 1.0000x reference)
import numpy as np
from contextlib import ExitStack
import concourse.bass as bass
import concourse.mybir as mybir
from concourse.bass_utils import run_bass_kernel_spmd

F32 = mybir.dt.float32
BF16 = mybir.dt.bfloat16
I32 = mybir.dt.int32
AF = mybir.ActivationFunctionType
ALU = mybir.AluOpType
AX = mybir.AxisListType

D = 1024
NH = 16
NJ = 8
DK = 64
DV = 64
CONV = 2048
INNER = 4096
C = 128
GT = 512
LN8 = float(np.log(8.0))
NEG = -1e30
DR = lambda e: e.drain()


class Sem:
    def __init__(self, nc, es, name):
        self.h = es.enter_context(nc.semaphore(name))
        self.v = 0


class KB:
    def __init__(self, nc, es):
        self.nc = nc
        self.ops = []
        self.q = Sem(nc, es, "sq")
        self.g = Sem(nc, es, "sg")
        self.gp = Sem(nc, es, "sgp")
        self.v = Sem(nc, es, "sv")
        self.s = Sem(nc, es, "ss")
        self.t = Sem(nc, es, "st")
        self.o = Sem(nc, es, "so")
        self.bc = Sem(nc, es, "sbc")
        self.ktv = Sem(nc, es, "sktv")
        self.hnf = Sem(nc, es, "shnf")
        self.x1s = Sem(nc, es, "sx1s")
        self.dbgS = Sem(nc, es, "sdbg")
        self.wd = [Sem(nc, es, "swd%d" % i) for i in range(4)]
        self.xs = [Sem(nc, es, "sxs0"), Sem(nc, es, "sxs1")]
        self.hs = [Sem(nc, es, "shs0"), Sem(nc, es, "shs1")]

    def E(self, eng, fn):
        self.ops.append((eng, fn))

    def Wv(self, eng, sem, val):
        if val > 0:
            self.E(eng, lambda e, _s=sem.h, _v=val: e.wait_ge(_s, _v))

    def W(self, eng, sem, lag=0):
        cv = sem.v - lag
        if cv > 0:
            self.E(eng, lambda e, _s=sem.h, _v=cv: e.wait_ge(_s, _v))

    def dma(self, out, in_, transpose=False, sem=None, eng='q'):
        sem = sem or self.q
        sem.v += 16
        if transpose:
            self.E(eng, lambda e, o=out, i=in_, s=sem.h: e.dma_start(
                out=o, in_=i, transpose=True).then_inc(s, 16))
        else:
            self.E(eng, lambda e, o=out, i=in_, s=sem.h: e.dma_start(
                out=o, in_=i).then_inc(s, 16))

    def gdma(self, out, in_):
        self.g.v += 16
        self.E('g', lambda e, o=out, i=in_: e.dma_start(out=o, in_=i)
               .then_inc(self.g.h, 16))

    def _batch(self, eng, sem, fns):
        for fn in fns[:-1]:
            self.E(eng, fn)
        sem.v += 1
        def _last(e, f=fns[-1], s=sem.h):
            r = f(e)
            if r is None:
                e.drain().then_inc(s, 1)
            else:
                r.then_inc(s, 1)
        self.E(eng, _last)

    def tb(self, *fns):
        self._batch('t', self.t, fns)

    def vb(self, *fns):
        self._batch('v', self.v, fns)

    def sb(self, *fns):
        self._batch('s', self.s, fns)

    def barrier(self):
        for eng in ['q', 'g', 'v', 's', 't']:
            for sem in [self.q, self.g, self.gp, self.v, self.s, self.t,
                        self.bc, self.ktv, self.hnf, self.x1s, self.dbgS]:
                self.W(eng, sem)

    def gb(self, *fns):
        for fn in fns[:-1]:
            self.E('g', fn)
        self.gp.v += 1
        self.E('g', lambda e, f=fns[-1]: f(e).then_inc(self.gp.h, 1))


def build_nc(seq_len=4096, dbg=False, phases='full'):
    NGR = seq_len // GT
    CPG = GT // C
    nc = bass.Bass("TRN2", target_bir_lowering=False)

    x_d = nc.dram_tensor("x", [seq_len, D], F32, kind="ExternalInput").ap()
    wqkv_d = nc.dram_tensor("wqkv", [CONV, D], F32, kind="ExternalInput").ap()
    convw_d = nc.dram_tensor("convw", [CONV, 4], F32, kind="ExternalInput").ap()
    wz_d = nc.dram_tensor("wz", [D, D], F32, kind="ExternalInput").ap()
    wba_d = nc.dram_tensor("wba", [32, D], F32, kind="ExternalInput").ap()
    dtb_d = nc.dram_tensor("dtb", [1, NH], F32, kind="ExternalInput").ap()
    alog_d = nc.dram_tensor("alog", [1, NH], F32, kind="ExternalInput").ap()
    nw_d = nc.dram_tensor("nw", [1, DV], F32, kind="ExternalInput").ap()
    wout_d = nc.dram_tensor("wout", [D, D], F32, kind="ExternalInput").ap()
    wg_d = nc.dram_tensor("wg", [INNER, D], F32, kind="ExternalInput").ap()
    wu_d = nc.dram_tensor("wu", [INNER, D], F32, kind="ExternalInput").ap()
    wd_d = nc.dram_tensor("wd", [D, INNER], F32, kind="ExternalInput").ap()
    out_d = nc.dram_tensor("out", [seq_len, D], F32, kind="ExternalOutput").ap()
    x1_d = nc.dram_tensor("x1s", [seq_len, D], F32, kind="Internal").ap()
    wdt_d = nc.dram_tensor("wdts", [INNER, D], BF16, kind="Internal").ap()
    bc_d = nc.dram_tensor("bcast", [1, 4096], F32, kind="Internal").ap()
    bc2_d = nc.dram_tensor("bcast2", [1, 64], F32, kind="Internal").ap()
    dbg_t = {}
    if dbg:
        for nm, sh in [("dbg_qkv", [128, 16 * 528]), ("dbg_g", [GT, NH]),
                       ("dbg_beta", [GT, NH]), ("dbg_o", [GT, D]),
                       ("dbg_x1", [GT, D]), ("dbg_z", [128, 4 * 1024]),
                       ("dbg_xq", [128, 1024]), ("dbg_h2", [128, 4096]),
                       ("dbg_scl", [128, 12]), ("dbg_srep", [128, 8]),
                       ("dbg_down", [128, 1024])]:
            dbg_t[nm] = nc.dram_tensor(nm, sh, F32, kind="ExternalOutput").ap()

    es = ExitStack()
    k = KB(nc, es)
    sbt = lambda name, shape, dt: es.enter_context(nc.sbuf_tensor(name, shape, dt))
    pst = lambda name, shape, dt: es.enter_context(nc.psum_tensor(name, shape, dt))

    identB = sbt("identB", [128, 128], BF16)
    ishB = sbt("ishB", [128, 128], BF16)
    identF = sbt("identF", [128, 128], F32)
    mask_incl = sbt("mask_incl", [128, 128], F32)
    mask_str = sbt("mask_str", [128, 128], F32)
    dtb_rep = sbt("dtb_rep", [128, NH], F32)
    negexpA = sbt("negexpA", [128, NH], F32)
    nw_rep = sbt("nw_rep", [128, DV], F32)
    onecol = sbt("onecol", [128, 1], F32)
    eps5c = sbt("eps5c", [128, 1], F32)
    eps6c = sbt("eps6c", [128, 1], F32)
    cstat = sbt("cstat", [128, 16], F32)

    pbig = pst("pbig", [128, 2048], F32)
    psolA = pst("psolA", [128, 512], F32)
    psolB = pst("psolB", [128, 512], F32)
    psolC = pst("psolC", [128, 512], F32)
    pmisc = pst("pmisc", [128, 512], F32)

    k.gb(lambda e: e.memset(identB[:], 0.0))
    k.W('g', k.gp)
    k.gb(lambda e: e.affine_select(out=identB[:], in_=identB[:], compare_op=ALU.not_equal,
                                   fill=1.0, base=0, pattern=[[-1, 128]], channel_multiplier=1))
    k.gb(lambda e: e.memset(ishB[:], 0.0))
    k.W('g', k.gp)
    k.gb(lambda e: e.affine_select(out=ishB[:], in_=ishB[:], compare_op=ALU.not_equal,
                                   fill=1.0, base=64, pattern=[[-1, 128]], channel_multiplier=1))
    k.gb(lambda e: e.memset(identF[:], 0.0))
    k.W('g', k.gp)
    k.gb(lambda e: e.affine_select(out=identF[:], in_=identF[:], compare_op=ALU.not_equal,
                                   fill=1.0, base=0, pattern=[[-1, 128]], channel_multiplier=1))
    k.gb(lambda e: e.memset(mask_incl[:], -LN8))
    k.W('g', k.gp)
    k.gb(lambda e: e.affine_select(out=mask_incl[:], in_=mask_incl[:], compare_op=ALU.is_ge,
                                   fill=NEG, base=0, pattern=[[1, 128]], channel_multiplier=-1))
    k.gb(lambda e: e.memset(mask_str[:], 0.0))
    k.W('g', k.gp)
    k.gb(lambda e: e.affine_select(out=mask_str[:], in_=mask_str[:], compare_op=ALU.is_gt,
                                   fill=NEG, base=0, pattern=[[1, 128]], channel_multiplier=-1))
    k.dma(dtb_rep[:], dtb_d.partition_broadcast(128).squeeze(1))
    k.dma(negexpA[:], alog_d.partition_broadcast(128).squeeze(1))
    k.dma(nw_rep[:], nw_d.partition_broadcast(128).squeeze(1))
    k.vb(lambda e: e.memset(onecol[:], 1.0),
         lambda e: e.memset(eps5c[:], 1e-5),
         lambda e: e.memset(eps6c[:], 1e-6))
    k.W('s', k.q)
    k.sb(lambda e: e.activation(out=negexpA[:], in_=negexpA[:], func=AF.Exp))
    k.W('v', k.s)
    k.vb(lambda e: e.tensor_scalar(out=negexpA[:], in0=negexpA[:], scalar1=-1.0,
                                   scalar2=None, op0=ALU.mult))

    esA = ExitStack()
    sA = lambda name, shape, dt: esA.enter_context(nc.sbuf_tensor(name, shape, dt))
    wqkvT = sA("wqkvT", [128, 8 * CONV], BF16)
    wzT = sA("wzT", [128, 8 * D], BF16)
    woutT = sA("woutT", [128, 8 * D], BF16)
    wbaT = sA("wbaT", [128, 8 * 32], BF16)
    diag = sA("diag", [128, 64 * 128], BF16)
    convw_sb = sA("convw_sb", [128, 64], F32)
    stg = sA("stg", [128, 1024], BF16)
    hnfm = stg
    stgba = sA("stgba", [32, 1024], BF16)
    xtm = sA("xtm", [128, 4 * 1024], F32)
    xbst = sA("xbst", [128, 4 * 1024], BF16)
    xfm = sA("xfm", [128, 8 * GT], BF16)
    qkvs = sA("qkvs", [128, 16 * 528], BF16)
    zs = sA("zs", [128, 4 * 1024], BF16)
    ktm = sA("ktm", [128, 2 * 512], BF16)
    kbh = sA("kbh", [128, 8 * 64], BF16)
    zqk = sA("zqk", [128, 8 * 128], BF16)
    zqseq = sA("zqseq", [128, 4 * 128], BF16)
    glb = sA("glb", [128, 4 * 32], F32)
    btm = sA("btm", [128, 4 * 16], F32)
    sp_t = sA("sp_t", [128, 16], F32)
    sp_t2 = sA("sp_t2", [128, 16], F32)
    fmg = sA("fmg", [16, 128], F32)
    fmg2 = sA("fmg2", [16, 128], F32)
    Gfm = sA("Gfm", [16, 128], F32)
    Hfm = sA("Hfm", [16, 128], F32)
    Gtm = sA("Gtm", [128, 16], F32)
    GCrep = sA("GCrep", [128, NH], F32)
    Lam = sA("Lam", [128, NH], F32)
    Lam8 = sA("Lam8", [128, 2 * NH], F32)
    gamC = sA("gamC", [128, 2 * NH], F32)
    EGc = sA("EGc", [128, 2 * NH], F32)
    bLam = sA("bLam", [128, NH], F32)
    repG = sA("repG", [128, 1024], F32)
    etmp = sA("etmp", [128, 1024], F32)
    pscr = sA("pscr", [128, 1024], F32)
    Einc = sA("Einc", [128, 2 * 2048], BF16)
    Ebs = sA("Ebs", [128, 2048], BF16)
    Mt = sA("Mt", [128, 2 * 2048], BF16)
    Ltl = sA("Ltl", [128, 2048], BF16)
    Ttl = Ebs
    XA = Einc
    pw = sA("pw", [128, 16 * 128], BF16)
    vtm = pw[:, 0:1024]
    Pfm = sA("Pfm", [128, 8 * 128], BF16)
    Utm = sA("Utm", [128, 8 * 64], BF16)
    otm = sA("otm", [128, 1024], F32)
    Ssb = sA("Ssb", [64, NH * DV], F32)
    Sbf = sA("Sbf", [128, NH * DV], BF16)
    Sb2 = sA("Sb2", [128, 8 * DV], BF16)
    x1t = otm

    def qsl(j, c0, c1):
        oc = j // 2
        p0 = (j % 2) * 64
        return qkvs[p0:p0 + 64, oc * 528 + 16 + c0: oc * 528 + 16 + c1]

    def ksl(j, c0, c1):
        oc = 4 + j // 2
        p0 = (j % 2) * 64
        return qkvs[p0:p0 + 64, oc * 528 + 16 + c0: oc * 528 + 16 + c1]

    def prep_weight(w_d, O, destT, dest_stride):
        for r in range(O // 128):
            k.W('g', k.q)
            k.gdma(stg[:], w_d[r * 128:(r + 1) * 128, :])
            k.W('q', k.g)
            for ic in range(8):
                k.dma(destT[:, ic * dest_stride + r * 128: ic * dest_stride + (r + 1) * 128],
                      stg[:, ic * 128:(ic + 1) * 128], transpose=True)

    prep_weight(wqkv_d, CONV, wqkvT, CONV)
    k.W('g', k.q)
    k.gdma(stgba[:], wba_d)
    k.W('q', k.g)
    for ic in range(8):
        k.dma(wbaT[:, ic * 32:(ic + 1) * 32], stgba[:, ic * 128:(ic + 1) * 128], transpose=True)
    for oc in range(16):
        k.dma(convw_sb[:, oc * 4:(oc + 1) * 4], convw_d[oc * 128:(oc + 1) * 128, :])
    k.W('v', k.q)
    k.W('v', k.gp)
    dv_ops = []
    for oc in range(16):
        for j in range(4):
            col = (oc * 4 + j) * 128
            dv_ops.append(lambda e, c=col, o=oc, jj=j: e.tensor_scalar(
                out=diag[:, c:c + 128], in0=identB[:],
                scalar1=convw_sb[:, o * 4 + jj:o * 4 + jj + 1], scalar2=None, op0=ALU.mult))
    dv_ops.append(lambda e: e.memset(qkvs[:], 0.0))
    dv_ops.append(lambda e: e.memset(Ssb[:], 0.0))
    dv_ops.append(lambda e: e.memset(Sbf[:], 0.0))
    dv_ops.append(lambda e: e.memset(Sb2[:], 0.0))
    dv_ops.append(lambda e: e.memset(zqk[:], 0.0))
    dv_ops.append(lambda e: e.memset(zqseq[:], 0.0))
    dv_ops.append(lambda e: e.memset(Pfm[:], 0.0))
    dv_ops.append(lambda e: e.memset(pw[:], 0.0))
    k.vb(*dv_ops)
    prep_weight(wz_d, D, wzT, D)
    prep_weight(wout_d, D, woutT, D)
    if phases == 'W':
        k.W('q', k.v)
        k.dma(out_d[0:128, 0:64], convw_sb[:])
        k.W('q', k.q)

    SD = {}

    def chvb(ch, *f):
        k.vb(*f)
        ch['v'] = k.v.v

    def chsb(ch, *f):
        k.sb(*f)
        ch['s'] = k.s.v

    def chtb(ch, *f):
        k.tb(*f)
        ch['t'] = k.t.v

    def chdma(ch, out, in_, **kw):
        k.dma(out, in_, **kw)
        ch['q'] = k.q.v

    for g in (range(NGR) if phases != 'W' else []):
        t0 = g * GT
        if g == 0 or phases != 'full':
            k.W('q', k.q)
            for tt in range(4):
                k.dma(xtm[:, tt * 1024:(tt + 1) * 1024],
                      x_d[t0 + tt * 128: t0 + (tt + 1) * 128, :])
            k.W('g', k.q)
            k.W('g', k.hnf)
            for tt in range(4):
                k.gdma(xbst[:, tt * 1024:(tt + 1) * 1024],
                       x_d[t0 + tt * 128: t0 + (tt + 1) * 128, :])
            k.W('q', k.g)
            k.W('q', k.t)
            for tt in range(4):
                for ic in range(8):
                    k.dma(xfm[:, ic * GT + tt * 128: ic * GT + (tt + 1) * 128],
                          xbst[:, tt * 1024 + ic * 128: tt * 1024 + (ic + 1) * 128],
                          transpose=True)

        ci_p = g * CPG - 1
        k.W('t', k.q)
        if g == 0 or phases != 'full':
            k.W('t', k.v)
        else:
            k.Wv('t', k.v, max(SD[('PDV', ci_p)], SD[('SEQV', ci_p)]))
            k.Wv('t', k.s, SD[('PDS', ci_p)])
        k.W('t', k.ktv)
        if g >= 1 and phases == 'full':
            k.Wv('s', k.v, SD[('SEQV', ci_p)])
            k.W('s', k.ktv)
        for oc in range(16):
            ps = pbig[:, (oc % 4) * 512:((oc % 4) + 1) * 512]
            if oc >= 4:
                k.W('t', k.s, lag=3)
            def mmix(e, oc=oc, ps=ps):
                for ic in range(8):
                    r = e.matmul(ps, lhsT=wqkvT[:, ic * CONV + oc * 128: ic * CONV + (oc + 1) * 128],
                                 rhs=xfm[:, ic * GT:(ic + 1) * GT],
                                 start=(ic == 0), stop=(ic == 7))
                return r
            k.tb(mmix)
            k.W('s', k.t)
            k.sb(lambda e, oc=oc, ps=ps: e.activation(
                out=qkvs[:, oc * 528 + 16: oc * 528 + 16 + GT], in_=ps, func=AF.Identity))
        for oc in range(16):
            ps = pbig[:, (oc % 4) * 512:((oc % 4) + 1) * 512]
            k.W('t', k.s, lag=3)
            def mconv(e, oc=oc, ps=ps):
                for j in range(4):
                    r = e.matmul(ps, lhsT=diag[:, (oc * 4 + j) * 128:(oc * 4 + j + 1) * 128],
                                 rhs=qkvs[:, oc * 528 + 13 + j: oc * 528 + 13 + j + GT],
                                 start=(j == 0), stop=(j == 3))
                return r
            k.tb(mconv)
            k.W('v', k.t)
            k.vb(lambda e, oc=oc: e.tensor_copy(out=qkvs[:, oc * 528 + 13: oc * 528 + 16],
                                                in_=qkvs[:, oc * 528 + 16 + 509: oc * 528 + 16 + 512]))
            k.W('s', k.t)
            k.W('s', k.v)
            k.sb(lambda e, oc=oc, ps=ps: e.activation(
                out=qkvs[:, oc * 528 + 16: oc * 528 + 16 + GT], in_=ps, func=AF.Silu))
        for tt in range(4):
            for zc in range(2):
                ps = psolA[:] if zc == 0 else psolB[:]
                if g >= 1 and phases == 'full' and tt == 0 and zc == 0:
                    k.Wv('t', k.v, SD[('POSTV5', ci_p)])
                k.W('t', k.s, lag=1)
                def mz(e, tt=tt, zc=zc, ps=ps):
                    for ic in range(8):
                        r = e.matmul(ps, lhsT=xfm[:, ic * GT + tt * 128: ic * GT + (tt + 1) * 128],
                                     rhs=wzT[:, ic * D + zc * 512: ic * D + (zc + 1) * 512],
                                     start=(ic == 0), stop=(ic == 7))
                    return r
                k.tb(mz)
                k.W('s', k.t)
                k.sb(lambda e, tt=tt, zc=zc, ps=ps: e.activation(
                    out=zs[:, tt * 1024 + zc * 512: tt * 1024 + (zc + 1) * 512],
                    in_=ps, func=AF.Silu))
        for tt in range(4):
            k.W('t', k.s)
            if g == 0 or phases != 'full':
                k.W('t', k.v)
            else:
                k.Wv('t', k.v, SD[('SEQV', ci_p)])
            def mba(e, tt=tt):
                for ic in range(8):
                    r = e.matmul(pmisc[:, 0:32],
                                 lhsT=xfm[:, ic * GT + tt * 128: ic * GT + (tt + 1) * 128],
                                 rhs=wbaT[:, ic * 32:(ic + 1) * 32],
                                 start=(ic == 0), stop=(ic == 7))
                return r
            k.tb(mba)
            bb = pmisc[:, 0:16]
            aa = pmisc[:, 16:32]
            gcol = glb[:, tt * 32: tt * 32 + 16]
            lcol = glb[:, tt * 32 + 16: tt * 32 + 32]
            bcol = btm[:, tt * 16:(tt + 1) * 16]
            k.W('v', k.t)
            k.vb(lambda e, aa=aa: e.tensor_add(out=sp_t[:], in0=aa, in1=dtb_rep[:]))
            k.W('s', k.v)
            k.sb(lambda e: e.activation(out=sp_t2[:], in_=sp_t[:], func=AF.Abs), DR,
                 lambda e: e.activation(out=sp_t2[:], in_=sp_t2[:], func=AF.Exp, scale=-1.0), DR,
                 lambda e: e.activation(out=sp_t2[:], in_=sp_t2[:], func=AF.Ln, bias=onecol[:, 0:1]), DR,
                 lambda e, gcol=gcol: e.activation(out=gcol, in_=sp_t[:], func=AF.Relu))
            k.W('v', k.s)
            k.vb(lambda e, gcol=gcol: e.tensor_add(out=gcol, in0=gcol, in1=sp_t2[:]), DR,
                 lambda e, gcol=gcol: e.tensor_mul(out=gcol, in0=gcol, in1=negexpA[:]))
            k.W('s', k.v)
            k.sb(lambda e, bb=bb: e.activation(out=sp_t[:], in_=bb, func=AF.Abs), DR,
                 lambda e: e.activation(out=sp_t[:], in_=sp_t[:], func=AF.Exp, scale=-1.0), DR,
                 lambda e: e.activation(out=sp_t[:], in_=sp_t[:], func=AF.Ln, bias=onecol[:, 0:1]), DR,
                 lambda e, bb=bb: e.activation(out=sp_t2[:], in_=bb, func=AF.Relu, scale=-1.0))
            k.W('v', k.s)
            k.vb(lambda e, lcol=lcol: e.tensor_add(out=lcol, in0=sp_t[:], in1=sp_t2[:]), DR,
                 lambda e, lcol=lcol: e.tensor_scalar(out=lcol, in0=lcol, scalar1=-1.0,
                                                      scalar2=None, op0=ALU.mult))
            k.W('s', k.v)
            k.sb(lambda e, lcol=lcol, bcol=bcol: e.activation(out=bcol, in_=lcol, func=AF.Exp))

        if dbg and g == 0:
            k.W('g', k.s)
            k.gdma(dbg_t["dbg_qkv"], qkvs[:])
            k.gdma(dbg_t["dbg_z"], zs[:])
            k.W('q', k.s)
            for tt in range(4):
                k.dma(dbg_t["dbg_g"][tt * 128:(tt + 1) * 128, :], glb[:, tt * 32:tt * 32 + 16])
                k.dma(dbg_t["dbg_beta"][tt * 128:(tt + 1) * 128, :], btm[:, tt * 16:(tt + 1) * 16])

        if phases == 'G':
            k.W('q', k.s)
            for tt in range(4):
                k.dma(out_d[t0 + tt * 128: t0 + (tt + 1) * 128, :],
                      xtm[:, tt * 1024:(tt + 1) * 1024])
            continue
        if phases == 'C1':
            pass
        grp = {'v': k.v.v, 's': k.s.v, 't': k.t.v, 'q': k.q.v}

        def pre_stages(g, tt, t0, grp):
            ci = g * CPG + tt
            po = tt % 2
            gcol = glb[:, tt * 32: tt * 32 + 16]
            lcol2 = glb[:, tt * 32 + 16:(tt + 1) * 32]
            bcol = btm[:, tt * 16:(tt + 1) * 16]
            Mt_ = Mt[:, po * 2048:(po + 1) * 2048]
            Ltl_ = Ltl[:, 0:2048]
            XA_ = Einc[:, po * 2048:(po + 1) * 2048]
            Einc_ = XA_
            Ebs_ = Ebs[:, 0:2048]
            Ttl_ = Ebs_
            ktm_ = ktm[:, po * 512:(po + 1) * 512]
            Lam8_ = Lam8[:, po * NH:(po + 1) * NH]
            gamC_ = gamC[:, po * NH:(po + 1) * NH]
            EGc_ = EGc[:, po * NH:(po + 1) * NH]
            pbbf = pbig[:].bitcast(BF16)
            ch = {}

            def p0():
                ch.update(v=max(grp['v'], SD.get(('PDV', ci - 1), 0)),
                          s=max(grp['s'], SD.get(('PDS', ci - 1), 0)),
                          t=max(grp['t'], SD.get(('PDT', ci - 1), 0)),
                          q=max(grp['q'], SD.get(('PDQ', ci - 1), 0)))
                k.Wv('t', k.v, ch['v'])
                k.Wv('t', k.s, ch['s'])
                chtb(ch, lambda e: e.transpose(pbig[0:16, 1536:1664], in_=gcol,
                                               identity=identF[:]),
                     lambda e: e.transpose(pbig[0:16, 1664:1792], in_=lcol2,
                                           identity=identF[:]))
                k.Wv('v', k.t, ch['t'])
                chvb(ch, lambda e: e.tensor_copy(out=fmg[:], in_=pbig[0:16, 1536:1664]),
                     lambda e: e.tensor_copy(out=fmg2[:], in_=pbig[0:16, 1664:1792]), DR,
                     lambda e: e.tensor_tensor_scan(out=Gfm[:], data0=fmg[:], data1=fmg[:],
                                                    initial=0.0, op0=ALU.add,
                                                    op1=ALU.bypass), DR,
                     lambda e: e.tensor_add(out=Hfm[:], in0=Gfm[:], in1=fmg2[:]))
                k.Wv('t', k.v, ch['v'])
                chtb(ch, lambda e: e.transpose(pbig[0:128, 1792:1808], in_=Gfm[:],
                                               identity=identF[0:16, 0:16]))
                k.Wv('v', k.t, ch['t'])
                chvb(ch, lambda e: e.tensor_copy(out=Gtm[:], in_=pbig[0:128, 1792:1808]))

            def p1():
                k.Wv('q', k.bc, k.bc.v)
                k.Wv('q', k.v, ch['v'])
                k.dma(bc_d[0:1, 0:2048].rearrange("a (h t) -> (a h) t", h=16), Gfm[:],
                      sem=k.bc)
                k.dma(bc_d[0:1, 2048:4096].rearrange("a (h t) -> (a h) t", h=16), Hfm[:],
                      sem=k.bc)
                k.dma(bc2_d[0:1, 0:16], Gtm[127:128, :], sem=k.bc)
                k.Wv('q', k.bc, k.bc.v)
                k.dma(GCrep[:], bc2_d[0:1, 0:16].partition_broadcast(128).squeeze(1),
                      sem=k.bc)
                k.Wv('s', k.bc, k.bc.v)
                k.Wv('s', k.v, max(ch['v'], SD.get(('SEQV', ci - 2), 0)))
                chsb(ch, lambda e: e.activation(out=Lam[:], in_=Gtm[:], func=AF.Exp),
                     lambda e: e.activation(out=gamC_, in_=GCrep[:], func=AF.Exp))
                k.Wv('v', k.s, ch['s'])
                chvb(ch, lambda e: e.tensor_scalar(out=Lam8_, in0=Lam[:], scalar1=0.125,
                                                   scalar2=None, op0=ALU.mult),
                     lambda e: e.tensor_sub(out=EGc_, in0=GCrep[:], in1=Gtm[:]),
                     lambda e: e.tensor_mul(out=bLam[:], in0=bcol, in1=Lam[:]))
                k.Wv('s', k.v, max(ch['v'], SD.get(('SEQV', ci - 2), 0)))
                chsb(ch, lambda e: e.activation(out=EGc_, in_=EGc_, func=AF.Exp))

            def mk_ehalf(src_off, Edst, msk, half, strict):
                def eh():
                    h0 = half * 8
                    k.Wv('q', k.v, ch['v'])
                    k.dma(repG[:],
                          bc_d[0:1, src_off + h0 * 128: src_off + (h0 + 8) * 128]
                          .partition_broadcast(128).squeeze(1), sem=k.bc)
                    k.Wv('v', k.bc, k.bc.v)
                    k.Wv('v', k.s, ch['s'])
                    chvb(ch, *[(lambda e, i=i, h0=h0, msk=msk: e.scalar_tensor_tensor(
                        out=etmp[:, i * 128:(i + 1) * 128],
                        in0=repG[:, i * 128:(i + 1) * 128],
                        scalar=Gtm[:, h0 + i:h0 + i + 1],
                        in1=msk[:],
                        op0=ALU.subtract, op1=ALU.add)) for i in range(8)])
                    if strict:
                        k.Wv('s', k.v, ch['v'])
                        k.Wv('s', k.t, SD.get(('PDT', ci - 1), 0))
                    else:
                        k.Wv('s', k.v, max(ch['v'], SD.get(('SEQV', ci - 2), 0)))
                        k.Wv('s', k.t, SD.get(('SEQT', ci - 2), 0))
                    chsb(ch, lambda e, Edst=Edst, h0=h0: e.activation(
                        out=Edst[:, h0 * 128:(h0 + 8) * 128], in_=etmp[:], func=AF.Exp))
                return eh

            def p4():
                k.Wv('t', k.s, ch['s'])
                k.Wv('v', k.t, ch['t'])
                chvb(ch, *[(lambda e, j=j, p0_=(j % 2) * 64: e.tensor_copy(
                    out=zqk[p0_:p0_ + 64, j * 128:(j + 1) * 128],
                    in_=ksl(j, tt * 128, (tt + 1) * 128))) for j in range(NJ)])
                k.Wv('t', k.v, ch['v'])
                mm1 = []
                for j in range(NJ):
                    qfull = qkvs[:, (j // 2) * 528 + 16 + tt * 128:
                                 (j // 2) * 528 + 16 + (tt + 1) * 128]
                    zk_t = zqk[:, j * 128:(j + 1) * 128]
                    mm1.append(lambda e, j=j, zk_t=zk_t, qfull=qfull: e.matmul(
                        pbig[:, j * 256: j * 256 + 128], lhsT=zk_t, rhs=qfull,
                        start=True, stop=True))
                    mm1.append(lambda e, j=j, zk_t=zk_t: e.matmul(
                        pbig[:, j * 256 + 128: j * 256 + 256], lhsT=zk_t, rhs=zk_t,
                        start=True, stop=True))
                chtb(ch, *mm1)

            def p5():
                k.Wv('v', k.t, max(ch['t'], SD.get(('SEQT', ci - 2), 0)))
                chvb(ch, *[(lambda e, h=h: e.tensor_mul(
                    out=Mt_[:, h * 128:(h + 1) * 128],
                    in0=pbig[:, (h // 2) * 256:(h // 2) * 256 + 128],
                    in1=Einc_[:, h * 128:(h + 1) * 128])) for h in range(NH)])
                chvb(ch, *[(lambda e, h=h: e.tensor_mul(
                    out=Ltl_[:, h * 128:(h + 1) * 128],
                    in0=pbig[:, (h // 2) * 256 + 128:(h // 2) * 256 + 256],
                    in1=Ebs_[:, h * 128:(h + 1) * 128])) for h in range(NH)])

            def p6():
                k.Wv('t', k.v, ch['v'])
                def ttr(e):
                    for h in range(NH):
                        r = e.transpose(pbbf[:, 2048 + h * 128: 2048 + (h + 1) * 128],
                                        in_=Ltl_[:, h * 128:(h + 1) * 128],
                                        identity=identB[:])
                    return r
                chtb(ch, ttr)
                k.Wv('v', k.t, ch['t'])
                chvb(ch, *[(lambda e, h=h: e.tensor_copy(
                    out=Ttl_[:, h * 128:(h + 1) * 128],
                    in_=pbbf[:, 2048 + h * 128: 2048 + (h + 1) * 128]))
                    for h in range(NH)])
                k.Wv('q', k.ktv, k.ktv.v)
                k.Wv('q', k.t, ch['t'])
                k.Wv('q', k.s, ch['s'])
                k.Wv('q', k.v, max(ch['v'], SD.get(('SEQV', ci - 2), 0)))
                for b in range(4):
                    k.dma(ktm_[:, b * 128:(b + 1) * 128],
                          qkvs[:, (4 + b) * 528 + 16 + tt * 128:
                               (4 + b) * 528 + 16 + (tt + 1) * 128], transpose=True,
                          sem=k.ktv)
                for b in range(8):
                    k.dma(vtm[:, b * 128:(b + 1) * 128],
                          qkvs[:, (8 + b) * 528 + 16 + tt * 128:
                               (8 + b) * 528 + 16 + (tt + 1) * 128], transpose=True,
                          sem=k.ktv)
                SD[('KTV', ci)] = k.ktv.v

            def p7():
                k.Wv('s', k.ktv, SD[('KTV', ci)])
                k.Wv('s', k.t, SD.get(('SEQT', ci - 2), 0))
                k.Wv('s', k.v, max(ch['v'], SD.get(('SEQV', ci - 2), 0)))
                xb_ops = []
                for h in range(NH):
                    vslc = vtm[:, (h // 2) * 128 + (h % 2) * 64:
                               (h // 2) * 128 + (h % 2) * 64 + 64]
                    kslc = ktm_[:, (h // 4) * 128 + ((h // 2) % 2) * 64:
                                (h // 4) * 128 + ((h // 2) % 2) * 64 + 64]
                    xb_ops.append(lambda e, h=h, vslc=vslc: e.activation(
                        out=XA_[:, h * 128: h * 128 + 64], in_=vslc,
                        func=AF.Identity, scale=bcol[:, h:h + 1]))
                    xb_ops.append(lambda e, h=h, kslc=kslc: e.activation(
                        out=XA_[:, h * 128 + 64: h * 128 + 128], in_=kslc,
                        func=AF.Identity, scale=bLam[:, h:h + 1]))
                chsb(ch, *xb_ops)

            def mk_solve(wv2):
                def solve():
                    hs = [8 * wv2 + i for i in range(8)]
                    xs = [XA_[:, h * 128:(h + 1) * 128] for h in hs]
                    k.Wv('t', k.v, ch['v'])
                    k.Wv('t', k.s, ch['s'])
                    chtb(ch, *[(lambda e, i=i, h=h: e.matmul(
                        pbig[:, i * 128:(i + 1) * 128],
                        lhsT=Ltl_[:, h * 128:(h + 1) * 128],
                        rhs=XA_[:, h * 128:(h + 1) * 128],
                        start=True, stop=True)) for i, h in enumerate(hs)])
                    k.Wv('v', k.t, ch['t'])
                    chvb(ch, *[(lambda e, i=i, x=x: e.tensor_sub(
                        out=x, in0=x, in1=pbig[:, i * 128:(i + 1) * 128]))
                        for i, x in enumerate(xs)])
                    for lev in range(6):
                        if lev == 0:
                            cus = [Ltl_[:, h * 128:(h + 1) * 128] for h in hs]
                            cvs = [Ttl_[:, h * 128:(h + 1) * 128] for h in hs]
                        else:
                            cus = [pw[:, (2 * i) * 128:(2 * i + 1) * 128] for i in range(8)]
                            cvs = [pw[:, (2 * i + 1) * 128:(2 * i + 2) * 128]
                                   for i in range(8)]
                        k.Wv('t', k.v, ch['v'])
                        if lev:
                            k.Wv('t', k.s, ch['s'])
                        chtb(ch, *([(lambda e, i=i, cu=cus[i], cv_=cvs[i]: e.matmul(
                            pbig[:, i * 128:(i + 1) * 128], lhsT=cv_, rhs=cu,
                            start=True, stop=True)) for i in range(8)] +
                            ([(lambda e, i=i, cu=cus[i], cv_=cvs[i]: e.matmul(
                            pbig[:, 1024 + i * 128: 1024 + (i + 1) * 128],
                            lhsT=cu, rhs=cv_,
                            start=True, stop=True)) for i in range(8)] if lev < 5 else [])))
                        k.Wv('v', k.t, ch['t'])
                        chvb(ch, *[(lambda e, i=i: e.tensor_copy(
                            out=pw[:, (2 * i) * 128:(2 * i + 1) * 128],
                            in_=pbig[:, i * 128:(i + 1) * 128])) for i in range(8)])
                        if lev < 5:
                            k.Wv('s', k.t, ch['t'])
                            chsb(ch, *[(lambda e, i=i: e.activation(
                                out=pw[:, (2 * i + 1) * 128:(2 * i + 2) * 128],
                                in_=pbig[:, 1024 + i * 128: 1024 + (i + 1) * 128],
                                func=AF.Identity)) for i in range(8)])
                        k.Wv('t', k.v, ch['v'])
                        chtb(ch, *[(lambda e, i=i, x=x: e.matmul(
                            pbig[:, i * 128:(i + 1) * 128],
                            lhsT=pw[:, (2 * i) * 128:(2 * i + 1) * 128], rhs=x,
                            start=True, stop=True)) for i, x in enumerate(xs)])
                        k.Wv('v', k.t, ch['t'])
                        chvb(ch, *[(lambda e, i=i, x=x: e.tensor_add(
                            out=x, in0=x,
                            in1=pbig[:, i * 128:(i + 1) * 128]))
                            for i, x in enumerate(xs)])
                    if wv2 == 1:
                        SD[('PDV', ci)] = ch['v']
                        SD[('PDT', ci)] = ch['t']
                        SD[('PDS', ci)] = ch['s']
                        SD[('PDQ', ci)] = ch['q']
                return solve

            return ([p0, p1,
                     mk_ehalf(0, Einc_, mask_incl, 0, False),
                     mk_ehalf(0, Einc_, mask_incl, 1, False),
                     mk_ehalf(2048, Ebs_, mask_str, 0, True),
                     mk_ehalf(2048, Ebs_, mask_str, 1, True),
                     p4, p5, p6, p7] + [mk_solve(wv2) for wv2 in range(2)])

        def seqpost_stages(g, tt, t0, grp):
            ci = g * CPG + tt
            po = tt % 2
            Mt_ = Mt[:, po * 2048:(po + 1) * 2048]
            XA_ = Einc[:, po * 2048:(po + 1) * 2048]
            ktm_ = ktm[:, po * 512:(po + 1) * 512]
            Lam8_ = Lam8[:, po * NH:(po + 1) * NH]
            gamC_ = gamC[:, po * NH:(po + 1) * NH]
            EGc_ = EGc[:, po * NH:(po + 1) * NH]
            sch = {}

            def mk_wave(wv2):
                def wave():
                    hs = [8 * wv2 + i for i in range(8)]
                    xs = [XA_[:, h * 128:(h + 1) * 128] for h in hs]
                    if wv2 == 0:
                        sch.update(
                            v=max(SD[('PDV', ci)], SD.get(('POSTV5', ci - 1), 0)),
                            s=max(SD[('PDS', ci)], SD.get(('POSTS', ci - 1), 0)),
                            t=max(SD[('PDT', ci)], SD.get(('POSTT', ci - 1), 0)),
                            q=SD[('PDQ', ci)])
                        k.Wv('v', k.q, grp['q'])
                        k.Wv('v', k.x1s, SD.get(('POSTQ', ci - 1), 0))
                        k.Wv('s', k.ktv, SD[('KTV', ci)])
                    k.Wv('t', k.v, sch['v'])
                    chtb(sch, *[(lambda e, i=i, x=x: e.transpose(
                        psolA[:].bitcast(BF16)[0:64, i * 128:(i + 1) * 128],
                        in_=x[:, 64:128], identity=identB[:])) for i, x in enumerate(xs)])
                    k.Wv('s', k.t, sch['t'])
                    k.Wv('s', k.v, sch['v'])
                    chsb(sch, *[(lambda e, i=i, h=h, kslc=ktm_[
                        :, (h // 4) * 128 + ((h // 2) % 2) * 64:
                        (h // 4) * 128 + ((h // 2) % 2) * 64 + 64]: e.activation(
                        out=kbh[:, i * 64:(i + 1) * 64], in_=kslc,
                        func=AF.Identity, scale=EGc_[:, h:h + 1]))
                        for i, h in enumerate(hs)])
                    k.Wv('v', k.t, sch['t'])
                    vb_ops = []
                    for i, h in enumerate(hs):
                        vb_ops.append(lambda e, i=i: e.tensor_copy(
                            out=Pfm[0:64, i * 128:(i + 1) * 128],
                            in_=psolA[:].bitcast(BF16)[0:64, i * 128:(i + 1) * 128]))
                    for jj in range(4 * wv2, 4 * wv2 + 4):
                        sl = jj % 4
                        p00 = (jj % 2) * 64
                        vb_ops.append(lambda e, jj=jj, sl=sl, p00=p00: e.tensor_copy(
                            out=zqseq[p00:p00 + 64, sl * 128:(sl + 1) * 128],
                            in_=qsl(jj, tt * 128, (tt + 1) * 128)))
                    chvb(sch, *vb_ops)
                    sch['v'] = max(sch['v'], SD.get(('wS', ci - 1, wv2), 0))
                    sch['s'] = max(sch['s'], SD.get(('wSs', ci - 1, wv2), 0))
                    k.Wv('t', k.v, sch['v'])
                    k.Wv('t', k.s, sch['s'])
                    chtb(sch, *[(lambda e, i=i, h=h: e.matmul(
                        psolB[:, i * 64:(i + 1) * 64],
                        lhsT=Pfm[:, i * 128:(i + 1) * 128],
                        rhs=Sbf[:, h * DV:(h + 1) * DV], start=True, stop=True))
                        for i, h in enumerate(hs)])
                    k.Wv('v', k.t, sch['t'])
                    chvb(sch, *[(lambda e, i=i, x=x: e.tensor_sub(
                        out=Utm[:, i * 64:(i + 1) * 64], in0=x[:, 0:64],
                        in1=psolB[:, i * 64:(i + 1) * 64])) for i, x in enumerate(xs)])
                    k.Wv('t', k.v, sch['v'])
                    mm_ops = []
                    for i, h in enumerate(hs):
                        jpar = (h // 2) % 2
                        slot = (h // 4) * 2 + (h % 2)
                        sl = (h // 2) % 4
                        Sq = (Sbf[:, h * DV:(h + 1) * DV] if jpar == 0
                              else Sb2[:, slot * DV:(slot + 1) * DV])
                        mm_ops.append(lambda e, i=i, h=h: e.matmul(
                            psolC[:, i * 64:(i + 1) * 64],
                            lhsT=Mt_[:, h * 128:(h + 1) * 128],
                            rhs=Utm[:, i * 64:(i + 1) * 64], start=True, stop=True))
                        mm_ops.append(lambda e, i=i, Sq=Sq, sl=sl: e.matmul(
                            psolA[:, i * 64:(i + 1) * 64],
                            lhsT=zqseq[:, sl * 128:(sl + 1) * 128],
                            rhs=Sq, start=True, stop=True))
                        mm_ops.append(lambda e, i=i: e.matmul(
                            pmisc[0:64, i * 64:(i + 1) * 64],
                            lhsT=kbh[:, i * 64:(i + 1) * 64],
                            rhs=Utm[:, i * 64:(i + 1) * 64], start=True, stop=True))
                    chtb(sch, *mm_ops)
                    if wv2 == 0:
                        k.Wv('s', k.v, SD.get(('POSTV', ci - 1), 0))
                        k.Wv('s', k.x1s, SD.get(('POSTQ', ci - 1), 0))
                        k.Wv('s', k.dbgS, k.dbgS.v)
                    k.Wv('s', k.t, sch['t'])
                    chsb(sch, *[(lambda e, i=i, h=h: e.activation(
                        out=otm[:, h * 64:(h + 1) * 64],
                        in_=psolA[:, i * 64:(i + 1) * 64],
                        func=AF.Identity, scale=Lam8_[:, h:h + 1]))
                        for i, h in enumerate(hs)])
                    k.Wv('v', k.t, sch['t'])
                    k.Wv('v', k.s, sch['s'])
                    vb2 = []
                    for i, h in enumerate(hs):
                        Ss_h = Ssb[:, h * DV:(h + 1) * DV]
                        vb2.append(lambda e, i=i, h=h: e.tensor_add(
                            out=otm[:, h * 64:(h + 1) * 64],
                            in0=otm[:, h * 64:(h + 1) * 64],
                            in1=psolC[:, i * 64:(i + 1) * 64]))
                        vb2.append(lambda e, i=i, h=h, Ss_h=Ss_h: e.scalar_tensor_tensor(
                            out=Ss_h, in0=Ss_h, scalar=gamC_[0:64, h:h + 1],
                            in1=pmisc[0:64, i * 64:(i + 1) * 64],
                            op0=ALU.mult, op1=ALU.add))
                    chvb(sch, *vb2)
                    k.Wv('s', k.v, sch['v'])
                    chsb(sch, *[(lambda e, h=h: e.activation(
                        out=Sbf[0:64, h * DV:(h + 1) * DV],
                        in_=Ssb[:, h * DV:(h + 1) * DV], func=AF.Identity)) for h in hs])
                    odd = [h for h in hs if (h // 2) % 2 == 1]
                    k.Wv('t', k.v, sch['v'])
                    k.Wv('t', k.s, sch['s'])
                    chtb(sch, *[(lambda e, i=i, h=h: e.matmul(
                        pmisc[0:128, i * 64:(i + 1) * 64], lhsT=ishB[:],
                        rhs=Sbf[:, h * DV:(h + 1) * DV], start=True, stop=True))
                        for i, h in enumerate(odd)])
                    k.Wv('v', k.t, sch['t'])
                    chvb(sch, *[(lambda e, i=i, h=h: e.tensor_copy(
                        out=Sb2[:, ((h // 4) * 2 + (h % 2)) * DV:
                                ((h // 4) * 2 + (h % 2) + 1) * DV],
                        in_=pmisc[0:128, i * 64:(i + 1) * 64]))
                        for i, h in enumerate(odd)])
                    SD[('wS', ci, wv2)] = sch['v']
                    SD[('wSs', ci, wv2)] = sch['s']
                    if wv2 == 1:
                        SD[('SEQV', ci)] = sch['v']
                        SD[('SEQT', ci)] = sch['t']
                return wave

            def s4():
                chvb(sch, lambda e: e.tensor_mul(out=otm[:], in0=otm[:],
                                                 in1=zs[:, tt * 1024:(tt + 1) * 1024]))
                k.Wv('s', k.v, max(sch['v'], SD.get(('POSTV', ci - 1), 0)))
                chsb(sch, lambda e: e.activation(out=pscr[:], in_=otm[:], func=AF.Square))
                k.Wv('v', k.s, sch['s'])
                chvb(sch, lambda e: e.tensor_reduce(
                    out=sp_t[:], in_=pscr[:].rearrange("p (h d) -> p h d", h=16),
                    axis=AX.X, op=ALU.add))
                k.Wv('s', k.v, sch['v'])
                chsb(sch, lambda e: e.activation(out=sp_t2[:], in_=sp_t[:], func=AF.Ln,
                                                 scale=1.0 / 64.0, bias=eps6c[:, 0:1]), DR,
                     lambda e: e.activation(out=sp_t2[:], in_=sp_t2[:], func=AF.Exp,
                                            scale=-0.5))
                k.Wv('v', k.s, sch['s'])
                chvb(sch, DR, lambda e: e.tensor_mul(
                    out=otm[:].rearrange("p (h d) -> p h d", h=16),
                    in0=otm[:].rearrange("p (h d) -> p h d", h=16),
                    in1=sp_t2[:].unsqueeze(2).broadcast_to([128, 16, 64])), DR)
                chvb(sch, lambda e: e.tensor_mul(
                    out=xbst[:, tt * 1024:(tt + 1) * 1024].rearrange(
                        "p (h d) -> p h d", h=16),
                    in0=otm[:].rearrange("p (h d) -> p h d", h=16),
                    in1=nw_rep[:].unsqueeze(1).broadcast_to([128, 16, 64])))

            def s5():
                if dbg and g == 0:
                    k.Wv('q', k.dbgS, k.dbgS.v)
                    k.Wv('q', k.v, sch['v'])
                    k.dma(dbg_t["dbg_o"][tt * 128:(tt + 1) * 128, :], otm[:], sem=k.dbgS)
                    k.Wv('v', k.dbgS, k.dbgS.v)
                k.Wv('q', k.hnf, k.hnf.v)
                k.Wv('q', k.v, sch['v'])
                k.Wv('q', k.t, SD.get(('POSTT', ci - 1), 0))
                for ic in range(8):
                    k.dma(hnfm[:, ic * 128:(ic + 1) * 128],
                          xbst[:, tt * 1024 + ic * 128: tt * 1024 + (ic + 1) * 128],
                          transpose=True, sem=k.hnf)
                for oc2 in range(2):
                    ps = psolA[:] if oc2 == 0 else psolB[:]
                    k.Wv('t', k.hnf, k.hnf.v)
                    k.Wv('t', k.v, sch['v'])
                    def mwo(e, oc2=oc2, ps=ps):
                        for vc in range(8):
                            r = e.matmul(ps, lhsT=hnfm[:, vc * 128:(vc + 1) * 128],
                                         rhs=woutT[:, vc * D + oc2 * 512:
                                                  vc * D + (oc2 + 1) * 512],
                                         start=(vc == 0), stop=(vc == 7))
                        return r
                    chtb(sch, mwo)
                    k.Wv('v', k.t, sch['t'])
                    chvb(sch, DR, lambda e, oc2=oc2, ps=ps: e.tensor_add(
                        out=x1t[:, oc2 * 512:(oc2 + 1) * 512], in0=ps,
                        in1=xtm[:, tt * 1024 + oc2 * 512: tt * 1024 + (oc2 + 1) * 512]))
                SD[('POSTT', ci)] = sch['t']
                SD[('POSTV5', ci)] = sch['v']

            def s6():
                k.Wv('v', k.s, max(sch['s'], SD.get(('POSTS', ci - 1), 0)))
                chvb(sch, DR,
                     lambda e: e.scalar_tensor_tensor(out=pscr[:], in0=x1t[:], scalar=1.0,
                                                      in1=x1t[:], op0=ALU.mult,
                                                      op1=ALU.mult), DR,
                     lambda e: e.tensor_reduce(out=cstat[:, 0:1], in_=x1t[:], axis=AX.X,
                                               op=ALU.add),
                     lambda e: e.tensor_reduce(out=cstat[:, 1:2], in_=pscr[:],
                                               axis=AX.X, op=ALU.add), DR,
                     lambda e: e.tensor_scalar(out=cstat[:, 2:3], in0=cstat[:, 0:1],
                                               scalar1=1.0 / D, scalar2=None, op0=ALU.mult),
                     lambda e: e.tensor_scalar(out=cstat[:, 4:5], in0=cstat[:, 1:2],
                                               scalar1=1.0 / D, scalar2=None,
                                               op0=ALU.mult), DR,
                     lambda e: e.tensor_mul(out=cstat[:, 3:4], in0=cstat[:, 2:3],
                                            in1=cstat[:, 2:3]), DR,
                     lambda e: e.tensor_sub(out=cstat[:, 5:6], in0=cstat[:, 4:5],
                                            in1=cstat[:, 3:4]))
                k.Wv('s', k.v, sch['v'])
                chsb(sch, lambda e: e.activation(out=cstat[:, 6:7], in_=cstat[:, 5:6],
                                                 func=AF.Ln, bias=eps5c[:, 0:1]), DR,
                     lambda e: e.activation(out=cstat[:, 6:7], in_=cstat[:, 6:7],
                                            func=AF.Exp, scale=-0.5))
                k.Wv('v', k.s, sch['s'])
                chvb(sch, lambda e: e.tensor_scalar(out=x1t[:], in0=x1t[:],
                                                    scalar1=cstat[:, 2:3],
                                                    scalar2=cstat[:, 6:7],
                                                    op0=ALU.subtract, op1=ALU.mult))
                if dbg and g == 0:
                    k.Wv('q', k.dbgS, k.dbgS.v)
                    k.Wv('q', k.v, sch['v'])
                    k.dma(dbg_t["dbg_x1"][tt * 128:(tt + 1) * 128, :], x1t[:], sem=k.dbgS)
                    k.Wv('v', k.dbgS, k.dbgS.v)
                k.Wv('q', k.x1s, k.x1s.v)
                k.Wv('q', k.v, sch['v'])
                k.dma(x1_d[t0 + tt * 128: t0 + (tt + 1) * 128, :], x1t[:], sem=k.x1s)
                if phases == 'A':
                    k.dma(out_d[t0 + tt * 128: t0 + (tt + 1) * 128, :], x1t[:], sem=k.x1s)
                SD[('POSTV', ci)] = sch['v']
                SD[('POSTS', ci)] = sch['s']
                SD[('POSTQ', ci)] = k.x1s.v

            return [mk_wave(wv2) for wv2 in range(2)] + [s4, s5, s6]

        pres = {tt: pre_stages(g, tt, t0, grp) for tt in range(CPG)}
        seqs = {tt: seqpost_stages(g, tt, t0, grp) for tt in range(CPG)}
        for st in pres[0]:
            st()
        def prefetch_x(tt):
            nt0 = (g + 1) * GT
            pv = SD[('POSTV', g * CPG + tt)]
            k.Wv('q', k.v, pv)
            k.dma(xtm[:, tt * 1024:(tt + 1) * 1024],
                  x_d[nt0 + tt * 128: nt0 + (tt + 1) * 128, :])
            k.Wv('g', k.hnf, k.hnf.v)
            k.Wv('g', k.v, pv)
            k.gdma(xbst[:, tt * 1024:(tt + 1) * 1024],
                   x_d[nt0 + tt * 128: nt0 + (tt + 1) * 128, :])
            k.Wv('q', k.g, k.g.v)
            k.Wv('q', k.t, grp['t'])
            for ic in range(8):
                k.dma(xfm[:, ic * GT + tt * 128: ic * GT + (tt + 1) * 128],
                      xbst[:, tt * 1024 + ic * 128: tt * 1024 + (ic + 1) * 128],
                      transpose=True)

        for tt in range(CPG):
            a = pres[tt + 1] if tt + 1 < CPG else []
            b = seqs[tt]
            ia = ib = 0
            na, nb = len(a), len(b)
            while ia < na or ib < nb:
                if ib < nb and (ia >= na or ib * max(na, 1) <= ia * nb):
                    b[ib]()
                    ib += 1
                else:
                    a[ia]()
                    ia += 1
            if phases == 'full' and g + 1 < NGR:
                prefetch_x(tt)

    esA.close()
    k.barrier()
    if phases in ('A', 'C1', 'C0', 'C0b', 'C0c', 'C0c2', 'C0d', 'C0e'):
        k.W('q', k.q)

    if phases != 'full':
        with es:
            with nc.Block() as blk:
                @blk.sync
                def _(e):
                    for eng, fn in k.ops:
                        if eng == 'q':
                            fn(e)

                @blk.gpsimd
                def _(e):
                    for eng, fn in k.ops:
                        if eng == 'g':
                            fn(e)

                @blk.vector
                def _(e):
                    for eng, fn in k.ops:
                        if eng == 'v':
                            fn(e)

                @blk.scalar
                def _(e):
                    for eng, fn in k.ops:
                        if eng == 's':
                            fn(e)

                @blk.tensor
                def _(e):
                    for eng, fn in k.ops:
                        if eng == 't':
                            fn(e)
        return nc

    esB = ExitStack()
    sB = lambda name, shape, dt: esB.enter_context(nc.sbuf_tensor(name, shape, dt))
    wgT = sB("wgT", [128, 8 * INNER], BF16)
    wuT = sB("wuT", [128, 8 * INNER], BF16)
    wdk = sB("wdk", [128, 3 * 2048], BF16)
    tfh = sB("tfh", [128, 1024], F32)
    stg2f = tfh[:, 0:1024]
    x1g = sB("x1g", [128, 2 * 1024], F32)
    xq = sB("xq", [128, 1024], BF16)
    xqfm = sB("xqfm", [128, 2 * 1024], BF16)
    h2 = sB("h2", [128, 4096], F32)
    hqfm = sB("hqfm", [128, 2 * 4096], BF16)
    sgs = sB("sgs", [128, 8], F32)
    srep = sB("srep", [128, 8], F32)
    scl = sB("scl", [128, 24], F32)
    ot2 = sB("ot2", [128, 1024], F32)
    stgd = h2[:].bitcast(BF16)[:, 0:4096]
    q32 = h2[:].bitcast(I32)
    tscr = sB("tscr", [128, 4096], BF16)
    hq = tscr

    k.vb(lambda e: e.memset(sgs[:], 0.0))

    hqf32 = hqfm[:].bitcast(F32)[:, 0:4096]

    def absmean_w(w_d, O, acc_col):
        wv = w_d if w_d.shape[1] == 1024 else w_d.rearrange("a (n b) -> (a n) b", b=1024)
        for r in range(O // 512):
            pp = r % 2
            stgf = h2[:].rearrange("p (a b) -> p a b", a=4) if pp == 0 \
                else hqf32.rearrange("p (a b) -> p a b", a=4)
            stgff = h2[:] if pp == 0 else hqf32
            if pp == 0:
                k.W('q', k.s)
                k.dma(stgf, wv[r * 512:(r + 1) * 512, :]
                      .rearrange("(a p) b -> p a b", p=128))
                k.W('s', k.q)
            else:
                k.W('g', k.s)
                k.gdma(stgf, wv[r * 512:(r + 1) * 512, :]
                       .rearrange("(a p) b -> p a b", p=128))
                k.W('s', k.g)
            k.W('s', k.v)
            k.sb(lambda e, stgff=stgff, pp=pp: e.activation(
                out=tscr[:], in_=stgff, func=AF.Abs, accum_out=sgs[:, 3 + pp:4 + pp]))
            k.W('v', k.s)
            k.vb(lambda e, acc_col=acc_col, pp=pp: e.tensor_add(
                out=sgs[:, acc_col:acc_col + 1], in0=sgs[:, acc_col:acc_col + 1],
                in1=sgs[:, 3 + pp:4 + pp]))

    absmean_w(wg_d, INNER, 0)
    absmean_w(wu_d, INNER, 1)
    absmean_w(wd_d, D * 4, 2)

    k.W('t', k.v)
    k.tb(lambda e: e.matmul(pmisc[0:1, 0:3], lhsT=onecol[:], rhs=sgs[:, 0:3],
                            start=True, stop=True))
    k.W('v', k.t)
    k.vb(lambda e: e.tensor_copy(out=sgs[0:1, 4:7], in_=pmisc[0:1, 0:3]))
    k.W('q', k.v)
    k.dma(bc2_d[0:1, 32:35], sgs[0:1, 4:7])
    k.W('q', k.q)
    k.dma(srep[:, 0:3], bc2_d[0:1, 32:35].partition_broadcast(128).squeeze(1))
    k.W('v', k.q)
    k.vb(lambda e: e.tensor_scalar(out=srep[:, 0:3], in0=srep[:, 0:3],
                                   scalar1=1.0 / (INNER * D), scalar2=1e-5,
                                   op0=ALU.mult, op1=ALU.max), DR,
         lambda e: e.reciprocal(out=srep[:, 3:6], in_=srep[:, 0:3]))
    pbbf2 = pbig[:].bitcast(BF16)

    def tern_w(w_d, destT, sc):
        for r4 in range(INNER // 512):
            pp = r4 % 2
            stgf = h2[:].rearrange("p (a b) -> p a b", a=4) if pp == 0 \
                else hqf32.rearrange("p (a b) -> p a b", a=4)
            stgff = h2[:] if pp == 0 else hqf32
            if pp == 0:
                k.W('q', k.v)
                k.dma(stgf, w_d[r4 * 512:(r4 + 1) * 512, :]
                      .rearrange("(a p) b -> p a b", p=128))
                k.W('v', k.q)
            else:
                k.W('g', k.v)
                k.gdma(stgf, w_d[r4 * 512:(r4 + 1) * 512, :]
                       .rearrange("(a p) b -> p a b", p=128))
                k.W('v', k.g)
            k.W('v', k.t)
            k.vb(lambda e, sc=sc, stgff=stgff: e.tensor_scalar(
                out=tscr[:], in0=stgff, scalar1=srep[:, sc:sc + 1], scalar2=0.5,
                op0=ALU.mult, op1=ALU.is_gt),
                 lambda e, sc=sc, stgff=stgff: e.tensor_scalar(
                out=wdk[:, 0:4096], in0=stgff, scalar1=srep[:, sc:sc + 1], scalar2=-0.5,
                op0=ALU.mult, op1=ALU.is_lt), DR,
                 lambda e: e.tensor_sub(out=tscr[:], in0=tscr[:], in1=wdk[:, 0:4096]))
            k.W('t', k.v)
            k.W('t', k.s)
            def ttw(e):
                for s_ in range(32):
                    r_ = e.transpose(pbbf2[:, s_ * 128:(s_ + 1) * 128],
                                     in_=tscr[:, s_ * 128:(s_ + 1) * 128],
                                     identity=identB[:])
                return r_
            k.tb(ttw)
            k.W('v', k.t)
            k.vb(*[(lambda e, a=a, ic=ic, r4=r4: e.tensor_copy(
                out=destT[:, ic * INNER + (r4 * 4 + a) * 128:
                         ic * INNER + (r4 * 4 + a) * 128 + 128],
                in_=pbbf2[:, (a * 8 + ic) * 128:(a * 8 + ic + 1) * 128]))
                for a in range(4) for ic in range(8)])

    tern_w(wg_d, wgT, 3)
    tern_w(wu_d, wuT, 4)
    for r in range(D // 128):
        k.W('q', k.v)
        k.W('q', k.s)
        k.dma(h2[:], wd_d[r * 128:(r + 1) * 128, :])
        k.W('v', k.q)
        k.W('v', k.t)
        k.vb(lambda e: e.tensor_scalar(out=tscr[:], in0=h2[:], scalar1=srep[:, 5:6],
                                       scalar2=0.5, op0=ALU.mult, op1=ALU.is_gt),
             lambda e: e.tensor_scalar(out=hqfm[:, 4096:8192], in0=h2[:], scalar1=srep[:, 5:6],
                                       scalar2=-0.5, op0=ALU.mult, op1=ALU.is_lt), DR,
             lambda e: e.tensor_sub(out=tscr[:], in0=tscr[:], in1=hqfm[:, 4096:8192]))
        k.W('t', k.v)
        k.W('t', k.s)
        def ttd(e):
            for kk in range(32):
                r_ = e.transpose(pbbf2[:, kk * 128:(kk + 1) * 128],
                                 in_=tscr[:, kk * 128:(kk + 1) * 128], identity=identB[:])
            return r_
        k.tb(ttd)
        k.W('v', k.t)
        k.W('v', k.q)
        k.W('s', k.t)
        k.W('s', k.q)
        k.vb(*[(lambda e, kk=kk: e.tensor_copy(
            out=hqfm[:, kk * 128:(kk + 1) * 128],
            in_=pbbf2[:, kk * 128:(kk + 1) * 128])) for kk in range(32)])
        k.W('q', k.v)
        k.W('q', k.s)
        k.dma(wdt_d[:, r * 128:(r + 1) * 128].rearrange("(a p) b -> p a b", p=128),
              hqfm[:, 0:4096].rearrange("p (a b) -> p a b", a=32))

    k.W('t', k.v)
    k.W('t', k.s)

    NTT = NGR * 4
    snap_tr = {}
    snap_txq = {}
    snap_txqT = {}
    snap_hqe = {}
    snap_trT = {}
    snap_gu = {}
    snap_dn = {}
    snap_post = {}
    snap_stt = {}
    snap_hq = {}
    wdv = {}
    for ti in range(NTT + 1):
        g, tt = divmod(ti, 4)
        if ti >= 1:
            if ti == 1:
                k.W('g', k.q)
            k.Wv('g', k.t, snap_dn.get(ti - 2, 0))
            for pg in range(3):
                k.dma(wdk[:, pg * 2048:(pg + 1) * 2048].rearrange("p (a b) -> p a b", a=2),
                      wdt_d[pg * 256:(pg + 1) * 256, :].rearrange("(a p) b -> p a b", p=128),
                      sem=k.wd[pg], eng='g')
                wdv[pg] = k.wd[pg].v
        if ti < NTT:
            k.W('q', k.v)
            k.dma(x1g[:, (ti % 2) * 1024:(ti % 2) * 1024 + 1024],
                  x1_d[ti * 128:(ti + 1) * 128, :])
            scb = 12 * (ti % 2)
            xsl = x1g[:, (ti % 2) * 1024:(ti % 2) * 1024 + 1024]
            tq1 = hqfm[:].bitcast(I32)[:, (ti % 2) * 2048:(ti % 2) * 2048 + 2048]
            k.W('v', k.q)
            k.Wv('v', k.t, snap_txqT.get(ti - 1, 0))
            if ti >= 2:
                k.Wv('v', k.v, snap_post.get(ti - 2, 0))
            k.vb(DR, lambda e, xsl=xsl, scb=scb: e.tensor_reduce(
                     out=scl[:, scb:scb + 1], in_=xsl, axis=AX.X,
                     op=ALU.max, apply_absolute_value=True), DR,
                 lambda e, scb=scb: e.tensor_scalar(out=scl[:, scb:scb + 1],
                     in0=scl[:, scb:scb + 1], scalar1=1e-5, scalar2=None, op0=ALU.max), DR,
                 lambda e, scb=scb: e.reciprocal(out=scl[:, scb + 1:scb + 2],
                     in_=scl[:, scb:scb + 1]), DR,
                 lambda e, scb=scb: e.tensor_scalar(out=scl[:, scb + 1:scb + 2],
                     in0=scl[:, scb + 1:scb + 2], scalar1=127.0, scalar2=None, op0=ALU.mult),
                 lambda e, scb=scb: e.tensor_scalar(out=scl[:, scb + 2:scb + 3],
                     in0=scl[:, scb:scb + 1], scalar1=1.0 / 127.0, scalar2=None, op0=ALU.mult), DR,
                 lambda e, scb=scb: e.tensor_mul(out=scl[:, scb + 3:scb + 4],
                     in0=scl[:, scb + 2:scb + 3], in1=srep[:, 0:1]),
                 lambda e, scb=scb: e.tensor_mul(out=scl[:, scb + 4:scb + 5],
                     in0=scl[:, scb + 2:scb + 3], in1=srep[:, 1:2]), DR,
                 lambda e, xsl=xsl, tq1=tq1, scb=scb: e.tensor_scalar(
                     out=tq1[:, 0:1024], in0=xsl, scalar1=scl[:, scb + 1:scb + 2],
                     scalar2=512.5, op0=ALU.mult, op1=ALU.add), DR,
                 lambda e, xsl=xsl, tq1=tq1, scb=scb: e.scalar_tensor_tensor(
                     out=tfh[:, 0:1024], in0=xsl, scalar=scl[:, scb + 1:scb + 2],
                     in1=tq1[:, 0:1024], op0=ALU.mult, op1=ALU.subtract), DR,
                 lambda e: e.tensor_scalar(out=tfh[:, 0:1024], in0=tfh[:, 0:1024],
                     scalar1=-512.5, scalar2=-1.0, op0=ALU.is_lt, op1=ALU.mult), DR,
                 lambda e, tq1=tq1: e.scalar_tensor_tensor(
                     out=xq[:], in0=tfh[:, 0:1024], scalar=-512.0, in1=tq1[:, 0:1024],
                     op0=ALU.add, op1=ALU.add))

        if ti >= 1:
            pi = ti - 1
            pscb = 12 * (pi % 2)
            k.Wv('t', k.v, snap_tr[pi])
            if pi >= 1:
                k.Wv('t', k.v, snap_post.get(pi - 1, 0))
            for grp in range(16):
                par = (grp % 3) * 2048
                k.Wv('t', k.wd[grp % 3], wdv[grp])
                def mdn(e, grp=grp, par=par, pi=pi):
                    for a in range(2):
                        kk = grp * 2 + a
                        for oc in range(2):
                            r = e.matmul(psolA[:] if oc == 0 else psolB[:],
                                         lhsT=hqfm[:, (pi % 2) * 4096 + kk * 128:
                                                   (pi % 2) * 4096 + (kk + 1) * 128],
                                         rhs=wdk[:, par + a * 1024 + oc * 512:
                                                 par + a * 1024 + (oc + 1) * 512],
                                         start=(kk == 0), stop=(kk == 31))
                    return r
                k.tb(mdn)
                if grp + 3 < 16:
                    k.W('g', k.t)
                    k.dma(wdk[:, par:par + 2048].rearrange("p (a b) -> p a b", a=2),
                          wdt_d[(grp + 3) * 256:(grp + 4) * 256, :]
                          .rearrange("(a p) b -> p a b", p=128),
                          sem=k.wd[grp % 3], eng='g')
                    wdv[grp + 3] = k.wd[grp % 3].v
            snap_down = k.t.v
            snap_dn[pi] = k.t.v
        if ti < NTT:
            k.W('t', k.v)
            def txq(e):
                for ic in range(8):
                    r = e.transpose(psolC[:].bitcast(BF16)[:, ic * 128:(ic + 1) * 128],
                                    in_=xq[:, ic * 128:(ic + 1) * 128], identity=identB[:])
                return r
            k.tb(txq)
            snap_txqT[ti] = k.t.v
            k.W('v', k.t)
            k.vb(*[(lambda e, ic=ic, ti=ti: e.tensor_copy(
                out=xqfm[:, (ti % 2) * 1024 + ic * 128:(ti % 2) * 1024 + (ic + 1) * 128],
                in_=psolC[:].bitcast(BF16)[:, ic * 128:(ic + 1) * 128])) for ic in range(8)])
            snap_txq[ti] = k.v.v
            k.Wv('t', k.v, snap_txq[ti])
            k.W('s', k.o)
            for cc in range(8):
                psg = pbig[:, ((2 * cc) % 4) * 512:(((2 * cc) % 4) + 1) * 512]
                psu = pbig[:, ((2 * cc + 1) % 4) * 512:(((2 * cc + 1) % 4) + 1) * 512]
                if cc >= 2:
                    k.Wv('t', k.v, snap_stt.get((ti, cc - 2), 0))
                elif ti >= 1:
                    k.Wv('t', k.v, max(snap_stt.get((ti - 1, 6 + cc), 0),
                                       snap_hqe.get(ti - 1, 0)))
                def mgu(e, cc=cc, psg=psg, psu=psu, ti=ti):
                    xb = (ti % 2) * 1024
                    for ic in range(8):
                        e.matmul(psg, lhsT=xqfm[:, xb + ic * 128: xb + (ic + 1) * 128],
                                 rhs=wgT[:, ic * INNER + cc * 512: ic * INNER + (cc + 1) * 512],
                                 start=(ic == 0), stop=(ic == 7))
                    for ic in range(8):
                        r = e.matmul(psu, lhsT=xqfm[:, xb + ic * 128: xb + (ic + 1) * 128],
                                     rhs=wuT[:, ic * INNER + cc * 512: ic * INNER + (cc + 1) * 512],
                                     start=(ic == 0), stop=(ic == 7))
                    return r
                k.tb(mgu)
                k.W('s', k.t)
                k.W('s', k.v)
                k.sb(lambda e, cc=cc, psg=psg, scb=scb: e.activation(
                    out=ot2[:, (cc % 2) * 512:(cc % 2) * 512 + 512], in_=psg, func=AF.Silu,
                    scale=scl[:, scb + 3:scb + 4]))
                k.W('v', k.s)
                k.vb(lambda e, cc=cc, psu=psu, scb=scb: e.scalar_tensor_tensor(
                    out=h2[:, cc * 512:(cc + 1) * 512], in0=psu, scalar=scl[:, scb + 4:scb + 5],
                    in1=ot2[:, (cc % 2) * 512:(cc % 2) * 512 + 512], op0=ALU.mult, op1=ALU.mult))
                snap_stt[(ti, cc)] = k.v.v
            snap_gu[ti] = k.t.v
            if ti >= 1:
                k.Wv('v', k.t, snap_trT[ti - 1])
            k.vb(DR, lambda e, scb=scb: e.tensor_reduce(out=scl[:, scb + 5:scb + 6], in_=h2[:],
                     axis=AX.X, op=ALU.max, apply_absolute_value=True), DR,
                 lambda e, scb=scb: e.tensor_scalar(out=scl[:, scb + 5:scb + 6],
                     in0=scl[:, scb + 5:scb + 6], scalar1=1e-5, scalar2=None, op0=ALU.max), DR,
                 lambda e, scb=scb: e.reciprocal(out=scl[:, scb + 6:scb + 7],
                     in_=scl[:, scb + 5:scb + 6]), DR,
                 lambda e, scb=scb: e.tensor_scalar(out=scl[:, scb + 6:scb + 7],
                     in0=scl[:, scb + 6:scb + 7], scalar1=127.0, scalar2=None, op0=ALU.mult))
            for w2 in range(4):
                hsl = h2[:, w2 * 1024:(w2 + 1) * 1024]
                k.vb(DR, lambda e, hsl=hsl, tq1=tq1, scb=scb: e.tensor_scalar(
                         out=tq1[:, 0:1024], in0=hsl, scalar1=scl[:, scb + 6:scb + 7],
                         scalar2=512.5, op0=ALU.mult, op1=ALU.add), DR,
                     lambda e, hsl=hsl, tq1=tq1, scb=scb: e.scalar_tensor_tensor(
                         out=tfh[:, 0:1024], in0=hsl, scalar=scl[:, scb + 6:scb + 7],
                         in1=tq1[:, 0:1024], op0=ALU.mult, op1=ALU.subtract), DR,
                     lambda e: e.tensor_scalar(out=tfh[:, 0:1024], in0=tfh[:, 0:1024],
                         scalar1=-512.5, scalar2=-1.0, op0=ALU.is_lt, op1=ALU.mult), DR,
                     lambda e, w2=w2, tq1=tq1: e.scalar_tensor_tensor(
                         out=hq[:, w2 * 1024:(w2 + 1) * 1024], in0=tfh[:, 0:1024],
                         scalar=-512.0, in1=tq1[:, 0:1024], op0=ALU.add, op1=ALU.add))
            k.vb(DR, lambda e, scb=scb: e.scalar_tensor_tensor(
                out=scl[:, scb + 7:scb + 8], in0=scl[:, scb + 5:scb + 6],
                scalar=1.0 / 127.0, in1=srep[:, 2:3], op0=ALU.mult, op1=ALU.mult))
            k.W('t', k.v)
            def tq(e, ti=ti):
                for kk in range(32):
                    r = e.transpose(pbbf2[:, kk * 128:(kk + 1) * 128],
                                    in_=hq[:, kk * 128:(kk + 1) * 128],
                                    identity=identB[:])
                return r
            k.tb(tq)
            snap_trT[ti] = k.t.v
            k.W('v', k.t)
            k.vb(*[(lambda e, kk=kk, ti=ti: e.tensor_copy(
                out=hqfm[:, (ti % 2) * 4096 + kk * 128:(ti % 2) * 4096 + (kk + 1) * 128],
                in_=pbbf2[:, kk * 128:(kk + 1) * 128])) for kk in range(32)])
            snap_tr[ti] = k.v.v
            snap_hqe[ti] = k.v.v
        if ti >= 1:
            pi = ti - 1
            pg, ptt = divmod(pi, 4)
            pscb = 12 * (pi % 2)
            k.Wv('v', k.t, snap_down)
            k.W('v', k.o)
            pxb = (pi % 2) * 1024
            k.vb(DR, lambda e, pxb=pxb, pscb=pscb: e.scalar_tensor_tensor(
                     out=ot2[:, 0:512], in0=psolA[:], scalar=scl[:, pscb + 7:pscb + 8],
                     in1=x1g[:, pxb: pxb + 512], op0=ALU.mult, op1=ALU.add),
                 lambda e, pxb=pxb, pscb=pscb: e.scalar_tensor_tensor(
                     out=ot2[:, 512:1024], in0=psolB[:], scalar=scl[:, pscb + 7:pscb + 8],
                     in1=x1g[:, pxb + 512: pxb + 1024],
                     op0=ALU.mult, op1=ALU.add), DR,
                 lambda e, pscb=pscb: e.scalar_tensor_tensor(out=tfh[:, 0:1024]
                     if False else h2[:, 0:1024], in0=ot2[:], scalar=1.0,
                     in1=ot2[:], op0=ALU.mult, op1=ALU.mult), DR,
                 lambda e, pscb=pscb: e.tensor_reduce(out=scl[:, pscb + 8:pscb + 9],
                     in_=ot2[:], axis=AX.X, op=ALU.add),
                 lambda e, pscb=pscb: e.tensor_reduce(out=scl[:, pscb + 9:pscb + 10],
                     in_=h2[:, 0:1024], axis=AX.X, op=ALU.add), DR,
                 lambda e, pscb=pscb: e.tensor_scalar(out=scl[:, pscb + 8:pscb + 9],
                     in0=scl[:, pscb + 8:pscb + 9], scalar1=1.0 / D, scalar2=None, op0=ALU.mult),
                 lambda e, pscb=pscb: e.tensor_scalar(out=scl[:, pscb + 9:pscb + 10],
                     in0=scl[:, pscb + 9:pscb + 10], scalar1=1.0 / D, scalar2=None,
                     op0=ALU.mult), DR,
                 lambda e, pscb=pscb: e.tensor_mul(out=scl[:, pscb + 10:pscb + 11],
                     in0=scl[:, pscb + 8:pscb + 9], in1=scl[:, pscb + 8:pscb + 9]), DR,
                 lambda e, pscb=pscb: e.tensor_sub(out=scl[:, pscb + 9:pscb + 10],
                     in0=scl[:, pscb + 9:pscb + 10], in1=scl[:, pscb + 10:pscb + 11]))
            k.W('s', k.v)
            k.sb(lambda e, pscb=pscb: e.activation(out=scl[:, pscb + 11:pscb + 12],
                     in_=scl[:, pscb + 9:pscb + 10], func=AF.Ln, bias=eps5c[:, 0:1]), DR,
                 lambda e, pscb=pscb: e.activation(out=scl[:, pscb + 11:pscb + 12],
                     in_=scl[:, pscb + 11:pscb + 12], func=AF.Exp, scale=-0.5))
            k.W('v', k.s)
            k.vb(lambda e, pscb=pscb: e.tensor_scalar(out=ot2[:], in0=ot2[:],
                     scalar1=scl[:, pscb + 8:pscb + 9], scalar2=scl[:, pscb + 11:pscb + 12],
                     op0=ALU.subtract, op1=ALU.mult))
            snap_post[pi] = k.v.v
            k.W('q', k.v)
            k.dma(out_d[pg * GT + ptt * 128: pg * GT + (ptt + 1) * 128, :], ot2[:], sem=k.o)

    esB.close()
    k.W('q', k.q)
    k.W('q', k.o)

    with es:
        with nc.Block() as blk:
            @blk.sync
            def _(e):
                for eng, fn in k.ops:
                    if eng == 'q':
                        fn(e)

            @blk.gpsimd
            def _(e):
                for eng, fn in k.ops:
                    if eng == 'g':
                        fn(e)

            @blk.vector
            def _(e):
                for eng, fn in k.ops:
                    if eng == 'v':
                        fn(e)

            @blk.scalar
            def _(e):
                for eng, fn in k.ops:
                    if eng == 's':
                        fn(e)

            @blk.tensor
            def _(e):
                for eng, fn in k.ops:
                    if eng == 't':
                        fn(e)
    return nc


def shard_inputs(inputs):
    x = np.ascontiguousarray(np.asarray(inputs["L0_attn_hidden_states"], np.float32))
    wba = np.ascontiguousarray(np.concatenate(
        [np.asarray(inputs["L0_attn_W_b"], np.float32),
         np.asarray(inputs["L0_attn_W_a"], np.float32)], axis=0))
    common = {
        "wqkv": np.ascontiguousarray(np.asarray(inputs["L0_attn_W_qkv"], np.float32)),
        "convw": np.ascontiguousarray(np.asarray(inputs["L0_attn_conv_w"], np.float32)
                                      .reshape(CONV, 4)),
        "wz": np.ascontiguousarray(np.asarray(inputs["L0_attn_W_z"], np.float32)),
        "wba": wba,
        "dtb": np.ascontiguousarray(np.asarray(inputs["L0_attn_dt_bias"], np.float32)
                                    .reshape(1, NH)),
        "alog": np.ascontiguousarray(np.asarray(inputs["L0_attn_A_log"], np.float32)
                                     .reshape(1, NH)),
        "nw": np.ascontiguousarray(np.asarray(inputs["L0_attn_norm_weight"], np.float32)
                                   .reshape(1, DV)),
        "wout": np.ascontiguousarray(np.asarray(inputs["L0_attn_W_out"], np.float32)),
        "wg": np.ascontiguousarray(np.asarray(inputs["L0_ffn_w_gate"], np.float32)),
        "wu": np.ascontiguousarray(np.asarray(inputs["L0_ffn_w_up"], np.float32)),
        "wd": np.ascontiguousarray(np.asarray(inputs["L0_ffn_w_down"], np.float32)),
    }
    return x, common


def kernel(**inputs):
    x, common = shard_inputs(inputs)
    B = x.shape[0]
    nc = build_nc(x.shape[1])
    in_maps = [dict(common, x=np.ascontiguousarray(x[i])) for i in range(B)]
    res = run_bass_kernel_spmd(nc, in_maps, core_ids=list(range(B)))
    return np.stack([r["out"] for r in res.results])


# version 23 (final)
# speedup vs baseline: 1.0124x; 1.0124x over previous
import numpy as np
from contextlib import ExitStack
import concourse.bass as bass
import concourse.mybir as mybir
from concourse.bass_utils import run_bass_kernel_spmd

F32 = mybir.dt.float32
BF16 = mybir.dt.bfloat16
I32 = mybir.dt.int32
AF = mybir.ActivationFunctionType
ALU = mybir.AluOpType
AX = mybir.AxisListType

D = 1024
NH = 16
NJ = 8
DK = 64
DV = 64
CONV = 2048
INNER = 4096
C = 128
GT = 512
LN8 = float(np.log(8.0))
NEG = -1e30
DR = lambda e: e.drain()


class Sem:
    def __init__(self, nc, es, name):
        self.h = es.enter_context(nc.semaphore(name))
        self.v = 0


class KB:
    def __init__(self, nc, es):
        self.nc = nc
        self.ops = []
        self.q = Sem(nc, es, "sq")
        self.g = Sem(nc, es, "sg")
        self.gp = Sem(nc, es, "sgp")
        self.v = Sem(nc, es, "sv")
        self.s = Sem(nc, es, "ss")
        self.t = Sem(nc, es, "st")
        self.o = Sem(nc, es, "so")
        self.bc = Sem(nc, es, "sbc")
        self.ktv = Sem(nc, es, "sktv")
        self.hnf = Sem(nc, es, "shnf")
        self.x1s = Sem(nc, es, "sx1s")
        self.dbgS = Sem(nc, es, "sdbg")
        self.wd = [Sem(nc, es, "swd%d" % i) for i in range(4)]
        self.xs = [Sem(nc, es, "sxs0"), Sem(nc, es, "sxs1")]
        self.hs = [Sem(nc, es, "shs0"), Sem(nc, es, "shs1")]

    def E(self, eng, fn):
        self.ops.append((eng, fn))

    def Wv(self, eng, sem, val):
        if val > 0:
            self.E(eng, lambda e, _s=sem.h, _v=val: e.wait_ge(_s, _v))

    def W(self, eng, sem, lag=0):
        cv = sem.v - lag
        if cv > 0:
            self.E(eng, lambda e, _s=sem.h, _v=cv: e.wait_ge(_s, _v))

    def dma(self, out, in_, transpose=False, sem=None, eng='q'):
        sem = sem or self.q
        sem.v += 16
        if transpose:
            self.E(eng, lambda e, o=out, i=in_, s=sem.h: e.dma_start(
                out=o, in_=i, transpose=True).then_inc(s, 16))
        else:
            self.E(eng, lambda e, o=out, i=in_, s=sem.h: e.dma_start(
                out=o, in_=i).then_inc(s, 16))

    def gdma(self, out, in_):
        self.g.v += 16
        self.E('g', lambda e, o=out, i=in_: e.dma_start(out=o, in_=i)
               .then_inc(self.g.h, 16))

    def _batch(self, eng, sem, fns):
        for fn in fns[:-1]:
            self.E(eng, fn)
        sem.v += 1
        def _last(e, f=fns[-1], s=sem.h):
            r = f(e)
            if r is None:
                e.drain().then_inc(s, 1)
            else:
                r.then_inc(s, 1)
        self.E(eng, _last)

    def tb(self, *fns):
        self._batch('t', self.t, fns)

    def vb(self, *fns):
        self._batch('v', self.v, fns)

    def sb(self, *fns):
        self._batch('s', self.s, fns)

    def barrier(self):
        for eng in ['q', 'g', 'v', 's', 't']:
            for sem in [self.q, self.g, self.gp, self.v, self.s, self.t,
                        self.bc, self.ktv, self.hnf, self.x1s, self.dbgS]:
                self.W(eng, sem)

    def gb(self, *fns):
        for fn in fns[:-1]:
            self.E('g', fn)
        self.gp.v += 1
        self.E('g', lambda e, f=fns[-1]: f(e).then_inc(self.gp.h, 1))


def build_nc(seq_len=4096, dbg=False, phases='full'):
    NGR = seq_len // GT
    CPG = GT // C
    nc = bass.Bass("TRN2", target_bir_lowering=False)

    x_d = nc.dram_tensor("x", [seq_len, D], F32, kind="ExternalInput").ap()
    wqkv_d = nc.dram_tensor("wqkv", [CONV, D], F32, kind="ExternalInput").ap()
    convw_d = nc.dram_tensor("convw", [CONV, 4], F32, kind="ExternalInput").ap()
    wz_d = nc.dram_tensor("wz", [D, D], F32, kind="ExternalInput").ap()
    wba_d = nc.dram_tensor("wba", [32, D], F32, kind="ExternalInput").ap()
    dtb_d = nc.dram_tensor("dtb", [1, NH], F32, kind="ExternalInput").ap()
    alog_d = nc.dram_tensor("alog", [1, NH], F32, kind="ExternalInput").ap()
    nw_d = nc.dram_tensor("nw", [1, DV], F32, kind="ExternalInput").ap()
    wout_d = nc.dram_tensor("wout", [D, D], F32, kind="ExternalInput").ap()
    wg_d = nc.dram_tensor("wg", [INNER, D], F32, kind="ExternalInput").ap()
    wu_d = nc.dram_tensor("wu", [INNER, D], F32, kind="ExternalInput").ap()
    wd_d = nc.dram_tensor("wd", [D, INNER], F32, kind="ExternalInput").ap()
    out_d = nc.dram_tensor("out", [seq_len, D], F32, kind="ExternalOutput").ap()
    x1_d = nc.dram_tensor("x1s", [seq_len, D], F32, kind="Internal").ap()
    wdt_d = nc.dram_tensor("wdts", [INNER, D], BF16, kind="Internal").ap()
    bc_d = nc.dram_tensor("bcast", [1, 4096], F32, kind="Internal").ap()
    bc2_d = nc.dram_tensor("bcast2", [1, 64], F32, kind="Internal").ap()
    dbg_t = {}
    if dbg:
        for nm, sh in [("dbg_qkv", [128, 16 * 528]), ("dbg_g", [GT, NH]),
                       ("dbg_beta", [GT, NH]), ("dbg_o", [GT, D]),
                       ("dbg_x1", [GT, D]), ("dbg_z", [128, 4 * 1024]),
                       ("dbg_xq", [128, 1024]), ("dbg_h2", [128, 4096]),
                       ("dbg_scl", [128, 12]), ("dbg_srep", [128, 8]),
                       ("dbg_down", [128, 1024])]:
            dbg_t[nm] = nc.dram_tensor(nm, sh, F32, kind="ExternalOutput").ap()

    es = ExitStack()
    k = KB(nc, es)
    sbt = lambda name, shape, dt: es.enter_context(nc.sbuf_tensor(name, shape, dt))
    pst = lambda name, shape, dt: es.enter_context(nc.psum_tensor(name, shape, dt))

    identB = sbt("identB", [128, 128], BF16)
    ishB = sbt("ishB", [128, 128], BF16)
    identF = sbt("identF", [128, 128], F32)
    mask_incl = sbt("mask_incl", [128, 128], F32)
    mask_str = sbt("mask_str", [128, 128], F32)
    dtb_rep = sbt("dtb_rep", [128, NH], F32)
    negexpA = sbt("negexpA", [128, NH], F32)
    nw_rep = sbt("nw_rep", [128, DV], F32)
    onecol = sbt("onecol", [128, 1], F32)
    eps5c = sbt("eps5c", [128, 1], F32)
    eps6c = sbt("eps6c", [128, 1], F32)
    cstat = sbt("cstat", [128, 16], F32)

    pbig = pst("pbig", [128, 2048], F32)
    psolA = pst("psolA", [128, 512], F32)
    psolB = pst("psolB", [128, 512], F32)
    psolC = pst("psolC", [128, 512], F32)
    pmisc = pst("pmisc", [128, 512], F32)

    k.gb(lambda e: e.memset(identB[:], 0.0))
    k.W('g', k.gp)
    k.gb(lambda e: e.affine_select(out=identB[:], in_=identB[:], compare_op=ALU.not_equal,
                                   fill=1.0, base=0, pattern=[[-1, 128]], channel_multiplier=1))
    k.gb(lambda e: e.memset(ishB[:], 0.0))
    k.W('g', k.gp)
    k.gb(lambda e: e.affine_select(out=ishB[:], in_=ishB[:], compare_op=ALU.not_equal,
                                   fill=1.0, base=64, pattern=[[-1, 128]], channel_multiplier=1))
    k.gb(lambda e: e.memset(identF[:], 0.0))
    k.W('g', k.gp)
    k.gb(lambda e: e.affine_select(out=identF[:], in_=identF[:], compare_op=ALU.not_equal,
                                   fill=1.0, base=0, pattern=[[-1, 128]], channel_multiplier=1))
    k.gb(lambda e: e.memset(mask_incl[:], -LN8))
    k.W('g', k.gp)
    k.gb(lambda e: e.affine_select(out=mask_incl[:], in_=mask_incl[:], compare_op=ALU.is_ge,
                                   fill=NEG, base=0, pattern=[[1, 128]], channel_multiplier=-1))
    k.gb(lambda e: e.memset(mask_str[:], 0.0))
    k.W('g', k.gp)
    k.gb(lambda e: e.affine_select(out=mask_str[:], in_=mask_str[:], compare_op=ALU.is_gt,
                                   fill=NEG, base=0, pattern=[[1, 128]], channel_multiplier=-1))
    k.dma(dtb_rep[:], dtb_d.partition_broadcast(128).squeeze(1))
    k.dma(negexpA[:], alog_d.partition_broadcast(128).squeeze(1))
    k.dma(nw_rep[:], nw_d.partition_broadcast(128).squeeze(1))
    k.vb(lambda e: e.memset(onecol[:], 1.0),
         lambda e: e.memset(eps5c[:], 1e-5),
         lambda e: e.memset(eps6c[:], 1e-6))
    k.W('s', k.q)
    k.sb(lambda e: e.activation(out=negexpA[:], in_=negexpA[:], func=AF.Exp))
    k.W('v', k.s)
    k.vb(lambda e: e.tensor_scalar(out=negexpA[:], in0=negexpA[:], scalar1=-1.0,
                                   scalar2=None, op0=ALU.mult))

    esA = ExitStack()
    sA = lambda name, shape, dt: esA.enter_context(nc.sbuf_tensor(name, shape, dt))
    wqkvT = sA("wqkvT", [128, 8 * CONV], BF16)
    wzT = sA("wzT", [128, 8 * D], BF16)
    woutT = sA("woutT", [128, 8 * D], BF16)
    wbaT = sA("wbaT", [128, 8 * 32], BF16)
    diag = sA("diag", [128, 64 * 128], BF16)
    convw_sb = sA("convw_sb", [128, 64], F32)
    stg = sA("stg", [128, 1024], BF16)
    hnfm = stg
    stgba = sA("stgba", [32, 1024], BF16)
    xtm = sA("xtm", [128, 4 * 1024], F32)
    xbst = sA("xbst", [128, 4 * 1024], BF16)
    xfm = sA("xfm", [128, 8 * GT], BF16)
    qkvs = sA("qkvs", [128, 16 * 528], BF16)
    zs = sA("zs", [128, 4 * 1024], BF16)
    ktm = sA("ktm", [128, 2 * 512], BF16)
    kbh = sA("kbh", [128, 8 * 64], BF16)
    zqk = sA("zqk", [128, 8 * 128], BF16)
    zqseq = sA("zqseq", [128, 4 * 128], BF16)
    glb = sA("glb", [128, 4 * 32], F32)
    btm = sA("btm", [128, 4 * 16], F32)
    sp_t = sA("sp_t", [128, 16], F32)
    sp_t2 = sA("sp_t2", [128, 16], F32)
    fmg = sA("fmg", [16, 128], F32)
    fmg2 = sA("fmg2", [16, 128], F32)
    Gfm = sA("Gfm", [16, 128], F32)
    Hfm = sA("Hfm", [16, 128], F32)
    Gtm = sA("Gtm", [128, 16], F32)
    GCrep = sA("GCrep", [128, NH], F32)
    Lam = sA("Lam", [128, NH], F32)
    Lam8 = sA("Lam8", [128, 2 * NH], F32)
    gamC = sA("gamC", [128, 2 * NH], F32)
    EGc = sA("EGc", [128, 2 * NH], F32)
    bLam = sA("bLam", [128, NH], F32)
    repG = sA("repG", [128, 1024], F32)
    etmp = sA("etmp", [128, 1024], F32)
    pscr = sA("pscr", [128, 1024], F32)
    Einc = sA("Einc", [128, 2 * 2048], BF16)
    Ebs = sA("Ebs", [128, 2048], BF16)
    Mt = sA("Mt", [128, 2 * 2048], BF16)
    Ltl = sA("Ltl", [128, 2048], BF16)
    Ttl = Ebs
    XA = Einc
    pw = sA("pw", [128, 16 * 128], BF16)
    vtm = pw[:, 0:1024]
    Pfm = sA("Pfm", [128, 8 * 128], BF16)
    Utm = sA("Utm", [128, 8 * 64], BF16)
    otm = sA("otm", [128, 1024], F32)
    Ssb = sA("Ssb", [64, NH * DV], F32)
    Sbf = sA("Sbf", [128, NH * DV], BF16)
    Sb2 = sA("Sb2", [128, 8 * DV], BF16)
    x1t = otm

    def qsl(j, c0, c1):
        oc = j // 2
        p0 = (j % 2) * 64
        return qkvs[p0:p0 + 64, oc * 528 + 16 + c0: oc * 528 + 16 + c1]

    def ksl(j, c0, c1):
        oc = 4 + j // 2
        p0 = (j % 2) * 64
        return qkvs[p0:p0 + 64, oc * 528 + 16 + c0: oc * 528 + 16 + c1]

    def prep_weight(w_d, O, destT, dest_stride):
        for r in range(O // 128):
            k.W('g', k.q)
            k.gdma(stg[:], w_d[r * 128:(r + 1) * 128, :])
            k.W('q', k.g)
            for ic in range(8):
                k.dma(destT[:, ic * dest_stride + r * 128: ic * dest_stride + (r + 1) * 128],
                      stg[:, ic * 128:(ic + 1) * 128], transpose=True)

    prep_weight(wqkv_d, CONV, wqkvT, CONV)
    k.W('g', k.q)
    k.gdma(stgba[:], wba_d)
    k.W('q', k.g)
    for ic in range(8):
        k.dma(wbaT[:, ic * 32:(ic + 1) * 32], stgba[:, ic * 128:(ic + 1) * 128], transpose=True)
    for oc in range(16):
        k.dma(convw_sb[:, oc * 4:(oc + 1) * 4], convw_d[oc * 128:(oc + 1) * 128, :])
    k.W('v', k.q)
    k.W('v', k.gp)
    dv_ops = []
    for oc in range(16):
        for j in range(4):
            col = (oc * 4 + j) * 128
            dv_ops.append(lambda e, c=col, o=oc, jj=j: e.tensor_scalar(
                out=diag[:, c:c + 128], in0=identB[:],
                scalar1=convw_sb[:, o * 4 + jj:o * 4 + jj + 1], scalar2=None, op0=ALU.mult))
    dv_ops.append(lambda e: e.memset(qkvs[:], 0.0))
    dv_ops.append(lambda e: e.memset(Ssb[:], 0.0))
    dv_ops.append(lambda e: e.memset(Sbf[:], 0.0))
    dv_ops.append(lambda e: e.memset(Sb2[:], 0.0))
    dv_ops.append(lambda e: e.memset(zqk[:], 0.0))
    dv_ops.append(lambda e: e.memset(zqseq[:], 0.0))
    dv_ops.append(lambda e: e.memset(Pfm[:], 0.0))
    dv_ops.append(lambda e: e.memset(pw[:], 0.0))
    k.vb(*dv_ops)
    prep_weight(wz_d, D, wzT, D)
    prep_weight(wout_d, D, woutT, D)
    if phases == 'W':
        k.W('q', k.v)
        k.dma(out_d[0:128, 0:64], convw_sb[:])
        k.W('q', k.q)

    SD = {}

    def chvb(ch, *f):
        k.vb(*f)
        ch['v'] = k.v.v

    def chsb(ch, *f):
        k.sb(*f)
        ch['s'] = k.s.v

    def chtb(ch, *f):
        k.tb(*f)
        ch['t'] = k.t.v

    def chdma(ch, out, in_, **kw):
        k.dma(out, in_, **kw)
        ch['q'] = k.q.v

    for g in (range(NGR) if phases != 'W' else []):
        t0 = g * GT
        if g == 0 or phases != 'full':
            k.W('q', k.q)
            for tt in range(4):
                k.dma(xtm[:, tt * 1024:(tt + 1) * 1024],
                      x_d[t0 + tt * 128: t0 + (tt + 1) * 128, :])
            k.W('g', k.q)
            k.W('g', k.hnf)
            for tt in range(4):
                k.gdma(xbst[:, tt * 1024:(tt + 1) * 1024],
                       x_d[t0 + tt * 128: t0 + (tt + 1) * 128, :])
            k.W('q', k.g)
            k.W('q', k.t)
            for tt in range(4):
                for ic in range(8):
                    k.dma(xfm[:, ic * GT + tt * 128: ic * GT + (tt + 1) * 128],
                          xbst[:, tt * 1024 + ic * 128: tt * 1024 + (ic + 1) * 128],
                          transpose=True)

        ci_p = g * CPG - 1
        k.W('t', k.q)
        if g == 0 or phases != 'full':
            k.W('t', k.v)
        else:
            k.Wv('t', k.v, max(SD[('PDV', ci_p)], SD[('SEQV', ci_p)]))
            k.Wv('t', k.s, SD[('PDS', ci_p)])
        k.W('t', k.ktv)
        if g >= 1 and phases == 'full':
            k.Wv('s', k.v, SD[('SEQV', ci_p)])
            k.W('s', k.ktv)
        for oc in range(16):
            ps = pbig[:, (oc % 4) * 512:((oc % 4) + 1) * 512]
            if oc >= 4:
                k.W('t', k.s, lag=3)
            def mmix(e, oc=oc, ps=ps):
                for ic in range(8):
                    r = e.matmul(ps, lhsT=wqkvT[:, ic * CONV + oc * 128: ic * CONV + (oc + 1) * 128],
                                 rhs=xfm[:, ic * GT:(ic + 1) * GT],
                                 start=(ic == 0), stop=(ic == 7))
                return r
            k.tb(mmix)
            k.W('s', k.t)
            k.sb(lambda e, oc=oc, ps=ps: e.activation(
                out=qkvs[:, oc * 528 + 16: oc * 528 + 16 + GT], in_=ps, func=AF.Identity))
        for oc in range(16):
            ps = pbig[:, (oc % 4) * 512:((oc % 4) + 1) * 512]
            k.W('t', k.s, lag=3)
            def mconv(e, oc=oc, ps=ps):
                for j in range(4):
                    r = e.matmul(ps, lhsT=diag[:, (oc * 4 + j) * 128:(oc * 4 + j + 1) * 128],
                                 rhs=qkvs[:, oc * 528 + 13 + j: oc * 528 + 13 + j + GT],
                                 start=(j == 0), stop=(j == 3))
                return r
            k.tb(mconv)
            k.W('v', k.t)
            k.vb(lambda e, oc=oc: e.tensor_copy(out=qkvs[:, oc * 528 + 13: oc * 528 + 16],
                                                in_=qkvs[:, oc * 528 + 16 + 509: oc * 528 + 16 + 512]))
            k.W('s', k.t)
            k.W('s', k.v)
            k.sb(lambda e, oc=oc, ps=ps: e.activation(
                out=qkvs[:, oc * 528 + 16: oc * 528 + 16 + GT], in_=ps, func=AF.Silu))
        for tt in range(4):
            for zc in range(2):
                ps = psolA[:] if zc == 0 else psolB[:]
                if g >= 1 and phases == 'full' and tt == 0 and zc == 0:
                    k.Wv('t', k.v, SD[('POSTV5', ci_p)])
                k.W('t', k.s, lag=1)
                def mz(e, tt=tt, zc=zc, ps=ps):
                    for ic in range(8):
                        r = e.matmul(ps, lhsT=xfm[:, ic * GT + tt * 128: ic * GT + (tt + 1) * 128],
                                     rhs=wzT[:, ic * D + zc * 512: ic * D + (zc + 1) * 512],
                                     start=(ic == 0), stop=(ic == 7))
                    return r
                k.tb(mz)
                k.W('s', k.t)
                k.sb(lambda e, tt=tt, zc=zc, ps=ps: e.activation(
                    out=zs[:, tt * 1024 + zc * 512: tt * 1024 + (zc + 1) * 512],
                    in_=ps, func=AF.Silu))
        for tt in range(4):
            k.W('t', k.s)
            if g == 0 or phases != 'full':
                k.W('t', k.v)
            else:
                k.Wv('t', k.v, SD[('SEQV', ci_p)])
            def mba(e, tt=tt):
                for ic in range(8):
                    r = e.matmul(pmisc[:, 0:32],
                                 lhsT=xfm[:, ic * GT + tt * 128: ic * GT + (tt + 1) * 128],
                                 rhs=wbaT[:, ic * 32:(ic + 1) * 32],
                                 start=(ic == 0), stop=(ic == 7))
                return r
            k.tb(mba)
            bb = pmisc[:, 0:16]
            aa = pmisc[:, 16:32]
            gcol = glb[:, tt * 32: tt * 32 + 16]
            lcol = glb[:, tt * 32 + 16: tt * 32 + 32]
            bcol = btm[:, tt * 16:(tt + 1) * 16]
            k.W('v', k.t)
            k.vb(lambda e, aa=aa: e.tensor_add(out=sp_t[:], in0=aa, in1=dtb_rep[:]))
            k.W('s', k.v)
            k.sb(lambda e: e.activation(out=sp_t2[:], in_=sp_t[:], func=AF.Abs), DR,
                 lambda e: e.activation(out=sp_t2[:], in_=sp_t2[:], func=AF.Exp, scale=-1.0), DR,
                 lambda e: e.activation(out=sp_t2[:], in_=sp_t2[:], func=AF.Ln, bias=onecol[:, 0:1]), DR,
                 lambda e, gcol=gcol: e.activation(out=gcol, in_=sp_t[:], func=AF.Relu))
            k.W('v', k.s)
            k.vb(lambda e, gcol=gcol: e.tensor_add(out=gcol, in0=gcol, in1=sp_t2[:]), DR,
                 lambda e, gcol=gcol: e.tensor_mul(out=gcol, in0=gcol, in1=negexpA[:]))
            k.W('s', k.v)
            k.sb(lambda e, bb=bb: e.activation(out=sp_t[:], in_=bb, func=AF.Abs), DR,
                 lambda e: e.activation(out=sp_t[:], in_=sp_t[:], func=AF.Exp, scale=-1.0), DR,
                 lambda e: e.activation(out=sp_t[:], in_=sp_t[:], func=AF.Ln, bias=onecol[:, 0:1]), DR,
                 lambda e, bb=bb: e.activation(out=sp_t2[:], in_=bb, func=AF.Relu, scale=-1.0))
            k.W('v', k.s)
            k.vb(lambda e, lcol=lcol: e.tensor_add(out=lcol, in0=sp_t[:], in1=sp_t2[:]), DR,
                 lambda e, lcol=lcol: e.tensor_scalar(out=lcol, in0=lcol, scalar1=-1.0,
                                                      scalar2=None, op0=ALU.mult))
            k.W('s', k.v)
            k.sb(lambda e, lcol=lcol, bcol=bcol: e.activation(out=bcol, in_=lcol, func=AF.Exp))

        if dbg and g == 0:
            k.W('g', k.s)
            k.gdma(dbg_t["dbg_qkv"], qkvs[:])
            k.gdma(dbg_t["dbg_z"], zs[:])
            k.W('q', k.s)
            for tt in range(4):
                k.dma(dbg_t["dbg_g"][tt * 128:(tt + 1) * 128, :], glb[:, tt * 32:tt * 32 + 16])
                k.dma(dbg_t["dbg_beta"][tt * 128:(tt + 1) * 128, :], btm[:, tt * 16:(tt + 1) * 16])

        if phases == 'G':
            k.W('q', k.s)
            for tt in range(4):
                k.dma(out_d[t0 + tt * 128: t0 + (tt + 1) * 128, :],
                      xtm[:, tt * 1024:(tt + 1) * 1024])
            continue
        if phases == 'C1':
            pass
        grp = {'v': k.v.v, 's': k.s.v, 't': k.t.v, 'q': k.q.v}

        def pre_stages(g, tt, t0, grp):
            ci = g * CPG + tt
            po = tt % 2
            gcol = glb[:, tt * 32: tt * 32 + 16]
            lcol2 = glb[:, tt * 32 + 16:(tt + 1) * 32]
            bcol = btm[:, tt * 16:(tt + 1) * 16]
            Mt_ = Mt[:, po * 2048:(po + 1) * 2048]
            Ltl_ = Ltl[:, 0:2048]
            XA_ = Einc[:, po * 2048:(po + 1) * 2048]
            Einc_ = XA_
            Ebs_ = Ebs[:, 0:2048]
            Ttl_ = Ebs_
            ktm_ = ktm[:, po * 512:(po + 1) * 512]
            Lam8_ = Lam8[:, po * NH:(po + 1) * NH]
            gamC_ = gamC[:, po * NH:(po + 1) * NH]
            EGc_ = EGc[:, po * NH:(po + 1) * NH]
            pbbf = pbig[:].bitcast(BF16)
            ch = {}

            def p0():
                ch.update(v=max(grp['v'], SD.get(('PDV', ci - 1), 0)),
                          s=max(grp['s'], SD.get(('PDS', ci - 1), 0)),
                          t=max(grp['t'], SD.get(('PDT', ci - 1), 0)),
                          q=max(grp['q'], SD.get(('PDQ', ci - 1), 0)))
                k.Wv('t', k.v, ch['v'])
                k.Wv('t', k.s, ch['s'])
                chtb(ch, lambda e: e.transpose(pbig[0:16, 1536:1664], in_=gcol,
                                               identity=identF[:]),
                     lambda e: e.transpose(pbig[0:16, 1664:1792], in_=lcol2,
                                           identity=identF[:]))
                k.Wv('v', k.t, ch['t'])
                chvb(ch, lambda e: e.tensor_copy(out=fmg[:], in_=pbig[0:16, 1536:1664]),
                     lambda e: e.tensor_copy(out=fmg2[:], in_=pbig[0:16, 1664:1792]), DR,
                     lambda e: e.tensor_tensor_scan(out=Gfm[:], data0=fmg[:], data1=fmg[:],
                                                    initial=0.0, op0=ALU.add,
                                                    op1=ALU.bypass), DR,
                     lambda e: e.tensor_add(out=Hfm[:], in0=Gfm[:], in1=fmg2[:]))
                k.Wv('t', k.v, ch['v'])
                chtb(ch, lambda e: e.transpose(pbig[0:128, 1792:1808], in_=Gfm[:],
                                               identity=identF[0:16, 0:16]))
                k.Wv('v', k.t, ch['t'])
                chvb(ch, lambda e: e.tensor_copy(out=Gtm[:], in_=pbig[0:128, 1792:1808]))

            def p1():
                k.Wv('q', k.bc, k.bc.v)
                k.Wv('q', k.v, ch['v'])
                k.dma(bc_d[0:1, 0:2048].rearrange("a (h t) -> (a h) t", h=16), Gfm[:],
                      sem=k.bc)
                k.dma(bc_d[0:1, 2048:4096].rearrange("a (h t) -> (a h) t", h=16), Hfm[:],
                      sem=k.bc)
                k.dma(bc2_d[0:1, 0:16], Gtm[127:128, :], sem=k.bc)
                k.Wv('q', k.bc, k.bc.v)
                k.dma(GCrep[:], bc2_d[0:1, 0:16].partition_broadcast(128).squeeze(1),
                      sem=k.bc)
                k.Wv('s', k.bc, k.bc.v)
                k.Wv('s', k.v, max(ch['v'], SD.get(('SEQV', ci - 2), 0)))
                chsb(ch, lambda e: e.activation(out=Lam[:], in_=Gtm[:], func=AF.Exp),
                     lambda e: e.activation(out=gamC_, in_=GCrep[:], func=AF.Exp))
                k.Wv('v', k.s, ch['s'])
                chvb(ch, lambda e: e.tensor_scalar(out=Lam8_, in0=Lam[:], scalar1=0.125,
                                                   scalar2=None, op0=ALU.mult),
                     lambda e: e.tensor_sub(out=EGc_, in0=GCrep[:], in1=Gtm[:]),
                     lambda e: e.tensor_mul(out=bLam[:], in0=bcol, in1=Lam[:]))
                k.Wv('s', k.v, max(ch['v'], SD.get(('SEQV', ci - 2), 0)))
                chsb(ch, lambda e: e.activation(out=EGc_, in_=EGc_, func=AF.Exp))

            def mk_ehalf(src_off, Edst, msk, half, strict):
                def eh():
                    h0 = half * 8
                    k.Wv('q', k.v, ch['v'])
                    k.dma(repG[:],
                          bc_d[0:1, src_off + h0 * 128: src_off + (h0 + 8) * 128]
                          .partition_broadcast(128).squeeze(1), sem=k.bc)
                    k.Wv('v', k.bc, k.bc.v)
                    k.Wv('v', k.s, ch['s'])
                    chvb(ch, *[(lambda e, i=i, h0=h0, msk=msk: e.scalar_tensor_tensor(
                        out=etmp[:, i * 128:(i + 1) * 128],
                        in0=repG[:, i * 128:(i + 1) * 128],
                        scalar=Gtm[:, h0 + i:h0 + i + 1],
                        in1=msk[:],
                        op0=ALU.subtract, op1=ALU.add)) for i in range(8)])
                    if strict:
                        k.Wv('s', k.v, ch['v'])
                        k.Wv('s', k.t, SD.get(('PDT', ci - 1), 0))
                    else:
                        k.Wv('s', k.v, max(ch['v'], SD.get(('SEQV', ci - 2), 0)))
                        k.Wv('s', k.t, SD.get(('SEQT', ci - 2), 0))
                    chsb(ch, lambda e, Edst=Edst, h0=h0: e.activation(
                        out=Edst[:, h0 * 128:(h0 + 8) * 128], in_=etmp[:], func=AF.Exp))
                return eh

            def p4():
                k.Wv('t', k.s, ch['s'])
                k.Wv('v', k.t, ch['t'])
                chvb(ch, *[(lambda e, j=j, p0_=(j % 2) * 64: e.tensor_copy(
                    out=zqk[p0_:p0_ + 64, j * 128:(j + 1) * 128],
                    in_=ksl(j, tt * 128, (tt + 1) * 128))) for j in range(NJ)])
                k.Wv('t', k.v, ch['v'])
                mm1 = []
                for j in range(NJ):
                    qfull = qkvs[:, (j // 2) * 528 + 16 + tt * 128:
                                 (j // 2) * 528 + 16 + (tt + 1) * 128]
                    zk_t = zqk[:, j * 128:(j + 1) * 128]
                    mm1.append(lambda e, j=j, zk_t=zk_t, qfull=qfull: e.matmul(
                        pbig[:, j * 256: j * 256 + 128], lhsT=zk_t, rhs=qfull,
                        start=True, stop=True))
                    mm1.append(lambda e, j=j, zk_t=zk_t: e.matmul(
                        pbig[:, j * 256 + 128: j * 256 + 256], lhsT=zk_t, rhs=zk_t,
                        start=True, stop=True))
                chtb(ch, *mm1)

            def p5():
                k.Wv('v', k.t, max(ch['t'], SD.get(('SEQT', ci - 2), 0)))
                chvb(ch, *[(lambda e, h=h: e.tensor_mul(
                    out=Mt_[:, h * 128:(h + 1) * 128],
                    in0=pbig[:, (h // 2) * 256:(h // 2) * 256 + 128],
                    in1=Einc_[:, h * 128:(h + 1) * 128])) for h in range(NH)])
                chvb(ch, *[(lambda e, h=h: e.tensor_mul(
                    out=Ltl_[:, h * 128:(h + 1) * 128],
                    in0=pbig[:, (h // 2) * 256 + 128:(h // 2) * 256 + 256],
                    in1=Ebs_[:, h * 128:(h + 1) * 128])) for h in range(NH)])

            def p6():
                k.Wv('t', k.v, ch['v'])
                def ttr(e):
                    for h in range(NH):
                        r = e.transpose(pbbf[:, 2048 + h * 128: 2048 + (h + 1) * 128],
                                        in_=Ltl_[:, h * 128:(h + 1) * 128],
                                        identity=identB[:])
                    return r
                chtb(ch, ttr)
                k.Wv('v', k.t, ch['t'])
                chvb(ch, *[(lambda e, h=h: e.tensor_copy(
                    out=Ttl_[:, h * 128:(h + 1) * 128],
                    in_=pbbf[:, 2048 + h * 128: 2048 + (h + 1) * 128]))
                    for h in range(NH)])
                k.Wv('q', k.ktv, k.ktv.v)
                k.Wv('q', k.t, ch['t'])
                k.Wv('q', k.s, ch['s'])
                k.Wv('q', k.v, max(ch['v'], SD.get(('SEQV', ci - 2), 0)))
                for b in range(4):
                    k.dma(ktm_[:, b * 128:(b + 1) * 128],
                          qkvs[:, (4 + b) * 528 + 16 + tt * 128:
                               (4 + b) * 528 + 16 + (tt + 1) * 128], transpose=True,
                          sem=k.ktv)
                for b in range(8):
                    k.dma(vtm[:, b * 128:(b + 1) * 128],
                          qkvs[:, (8 + b) * 528 + 16 + tt * 128:
                               (8 + b) * 528 + 16 + (tt + 1) * 128], transpose=True,
                          sem=k.ktv)
                SD[('KTV', ci)] = k.ktv.v

            def p7():
                k.Wv('s', k.ktv, SD[('KTV', ci)])
                k.Wv('s', k.t, SD.get(('SEQT', ci - 2), 0))
                k.Wv('s', k.v, max(ch['v'], SD.get(('SEQV', ci - 2), 0)))
                xb_ops = []
                for h in range(NH):
                    vslc = vtm[:, (h // 2) * 128 + (h % 2) * 64:
                               (h // 2) * 128 + (h % 2) * 64 + 64]
                    kslc = ktm_[:, (h // 4) * 128 + ((h // 2) % 2) * 64:
                                (h // 4) * 128 + ((h // 2) % 2) * 64 + 64]
                    xb_ops.append(lambda e, h=h, vslc=vslc: e.activation(
                        out=XA_[:, h * 128: h * 128 + 64], in_=vslc,
                        func=AF.Identity, scale=bcol[:, h:h + 1]))
                    xb_ops.append(lambda e, h=h, kslc=kslc: e.activation(
                        out=XA_[:, h * 128 + 64: h * 128 + 128], in_=kslc,
                        func=AF.Identity, scale=bLam[:, h:h + 1]))
                chsb(ch, *xb_ops)

            def mk_solve(wv2):
                def solve():
                    hs = [8 * wv2 + i for i in range(8)]
                    xs = [XA_[:, h * 128:(h + 1) * 128] for h in hs]
                    k.Wv('t', k.v, ch['v'])
                    k.Wv('t', k.s, ch['s'])
                    chtb(ch, *[(lambda e, i=i, h=h: e.matmul(
                        pbig[:, i * 128:(i + 1) * 128],
                        lhsT=Ltl_[:, h * 128:(h + 1) * 128],
                        rhs=XA_[:, h * 128:(h + 1) * 128],
                        start=True, stop=True)) for i, h in enumerate(hs)])
                    k.Wv('v', k.t, ch['t'])
                    chvb(ch, *[(lambda e, i=i, x=x: e.tensor_sub(
                        out=x, in0=x, in1=pbig[:, i * 128:(i + 1) * 128]))
                        for i, x in enumerate(xs)])
                    for lev in range(6):
                        if lev == 0:
                            cus = [Ltl_[:, h * 128:(h + 1) * 128] for h in hs]
                            cvs = [Ttl_[:, h * 128:(h + 1) * 128] for h in hs]
                        else:
                            cus = [pw[:, (2 * i) * 128:(2 * i + 1) * 128] for i in range(8)]
                            cvs = [pw[:, (2 * i + 1) * 128:(2 * i + 2) * 128]
                                   for i in range(8)]
                        k.Wv('t', k.v, ch['v'])
                        if lev:
                            k.Wv('t', k.s, ch['s'])
                        chtb(ch, *([(lambda e, i=i, cu=cus[i], cv_=cvs[i]: e.matmul(
                            pbig[:, i * 128:(i + 1) * 128], lhsT=cv_, rhs=cu,
                            start=True, stop=True)) for i in range(8)] +
                            ([(lambda e, i=i, cu=cus[i], cv_=cvs[i]: e.matmul(
                            pbig[:, 1024 + i * 128: 1024 + (i + 1) * 128],
                            lhsT=cu, rhs=cv_,
                            start=True, stop=True)) for i in range(8)] if lev < 5 else [])))
                        k.Wv('v', k.t, ch['t'])
                        chvb(ch, *[(lambda e, i=i: e.tensor_copy(
                            out=pw[:, (2 * i) * 128:(2 * i + 1) * 128],
                            in_=pbig[:, i * 128:(i + 1) * 128])) for i in range(8)])
                        if lev < 5:
                            k.Wv('s', k.t, ch['t'])
                            chsb(ch, *[(lambda e, i=i: e.activation(
                                out=pw[:, (2 * i + 1) * 128:(2 * i + 2) * 128],
                                in_=pbig[:, 1024 + i * 128: 1024 + (i + 1) * 128],
                                func=AF.Identity)) for i in range(8)])
                        k.Wv('t', k.v, ch['v'])
                        chtb(ch, *[(lambda e, i=i, x=x: e.matmul(
                            pbig[:, i * 128:(i + 1) * 128],
                            lhsT=pw[:, (2 * i) * 128:(2 * i + 1) * 128], rhs=x,
                            start=True, stop=True)) for i, x in enumerate(xs)])
                        k.Wv('v', k.t, ch['t'])
                        chvb(ch, *[(lambda e, i=i, x=x: e.tensor_add(
                            out=x, in0=x,
                            in1=pbig[:, i * 128:(i + 1) * 128]))
                            for i, x in enumerate(xs)])
                    if wv2 == 1:
                        SD[('PDV', ci)] = ch['v']
                        SD[('PDT', ci)] = ch['t']
                        SD[('PDS', ci)] = ch['s']
                        SD[('PDQ', ci)] = ch['q']
                return solve

            return ([p0, p1,
                     mk_ehalf(0, Einc_, mask_incl, 0, False),
                     mk_ehalf(0, Einc_, mask_incl, 1, False),
                     mk_ehalf(2048, Ebs_, mask_str, 0, True),
                     mk_ehalf(2048, Ebs_, mask_str, 1, True),
                     p4, p5, p6, p7] + [mk_solve(wv2) for wv2 in range(2)])

        def seqpost_stages(g, tt, t0, grp):
            ci = g * CPG + tt
            po = tt % 2
            Mt_ = Mt[:, po * 2048:(po + 1) * 2048]
            XA_ = Einc[:, po * 2048:(po + 1) * 2048]
            ktm_ = ktm[:, po * 512:(po + 1) * 512]
            Lam8_ = Lam8[:, po * NH:(po + 1) * NH]
            gamC_ = gamC[:, po * NH:(po + 1) * NH]
            EGc_ = EGc[:, po * NH:(po + 1) * NH]
            sch = {}

            def mk_wave(wv2):
                def wave():
                    hs = [8 * wv2 + i for i in range(8)]
                    xs = [XA_[:, h * 128:(h + 1) * 128] for h in hs]
                    if wv2 == 0:
                        sch.update(
                            v=max(SD[('PDV', ci)], SD.get(('POSTV5', ci - 1), 0)),
                            s=max(SD[('PDS', ci)], SD.get(('POSTS', ci - 1), 0)),
                            t=max(SD[('PDT', ci)], SD.get(('POSTT', ci - 1), 0)),
                            q=SD[('PDQ', ci)])
                        k.Wv('v', k.q, grp['q'])
                        k.Wv('v', k.x1s, SD.get(('POSTQ', ci - 1), 0))
                        k.Wv('s', k.ktv, SD[('KTV', ci)])
                    k.Wv('t', k.v, sch['v'])
                    chtb(sch, *[(lambda e, i=i, x=x: e.transpose(
                        psolA[:].bitcast(BF16)[0:64, i * 128:(i + 1) * 128],
                        in_=x[:, 64:128], identity=identB[:])) for i, x in enumerate(xs)])
                    k.Wv('s', k.t, sch['t'])
                    k.Wv('s', k.v, sch['v'])
                    chsb(sch, *[(lambda e, i=i, h=h, kslc=ktm_[
                        :, (h // 4) * 128 + ((h // 2) % 2) * 64:
                        (h // 4) * 128 + ((h // 2) % 2) * 64 + 64]: e.activation(
                        out=kbh[:, i * 64:(i + 1) * 64], in_=kslc,
                        func=AF.Identity, scale=EGc_[:, h:h + 1]))
                        for i, h in enumerate(hs)])
                    k.Wv('v', k.t, sch['t'])
                    vb_ops = []
                    for i, h in enumerate(hs):
                        vb_ops.append(lambda e, i=i: e.tensor_copy(
                            out=Pfm[0:64, i * 128:(i + 1) * 128],
                            in_=psolA[:].bitcast(BF16)[0:64, i * 128:(i + 1) * 128]))
                    for jj in range(4 * wv2, 4 * wv2 + 4):
                        sl = jj % 4
                        p00 = (jj % 2) * 64
                        vb_ops.append(lambda e, jj=jj, sl=sl, p00=p00: e.tensor_copy(
                            out=zqseq[p00:p00 + 64, sl * 128:(sl + 1) * 128],
                            in_=qsl(jj, tt * 128, (tt + 1) * 128)))
                    chvb(sch, *vb_ops)
                    sch['v'] = max(sch['v'], SD.get(('wS', ci - 1, wv2), 0))
                    sch['s'] = max(sch['s'], SD.get(('wSs', ci - 1, wv2), 0))
                    k.Wv('t', k.v, sch['v'])
                    k.Wv('t', k.s, sch['s'])
                    chtb(sch, *[(lambda e, i=i, h=h: e.matmul(
                        psolB[:, i * 64:(i + 1) * 64],
                        lhsT=Pfm[:, i * 128:(i + 1) * 128],
                        rhs=Sbf[:, h * DV:(h + 1) * DV], start=True, stop=True))
                        for i, h in enumerate(hs)])
                    k.Wv('v', k.t, sch['t'])
                    chvb(sch, *[(lambda e, i=i, x=x: e.tensor_sub(
                        out=Utm[:, i * 64:(i + 1) * 64], in0=x[:, 0:64],
                        in1=psolB[:, i * 64:(i + 1) * 64])) for i, x in enumerate(xs)])
                    k.Wv('t', k.v, sch['v'])
                    mm_ops = []
                    for i, h in enumerate(hs):
                        jpar = (h // 2) % 2
                        slot = (h // 4) * 2 + (h % 2)
                        sl = (h // 2) % 4
                        Sq = (Sbf[:, h * DV:(h + 1) * DV] if jpar == 0
                              else Sb2[:, slot * DV:(slot + 1) * DV])
                        mm_ops.append(lambda e, i=i, h=h: e.matmul(
                            psolC[:, i * 64:(i + 1) * 64],
                            lhsT=Mt_[:, h * 128:(h + 1) * 128],
                            rhs=Utm[:, i * 64:(i + 1) * 64], start=True, stop=True))
                        mm_ops.append(lambda e, i=i, Sq=Sq, sl=sl: e.matmul(
                            psolA[:, i * 64:(i + 1) * 64],
                            lhsT=zqseq[:, sl * 128:(sl + 1) * 128],
                            rhs=Sq, start=True, stop=True))
                        mm_ops.append(lambda e, i=i: e.matmul(
                            pmisc[0:64, i * 64:(i + 1) * 64],
                            lhsT=kbh[:, i * 64:(i + 1) * 64],
                            rhs=Utm[:, i * 64:(i + 1) * 64], start=True, stop=True))
                    chtb(sch, *mm_ops)
                    if wv2 == 0:
                        k.Wv('s', k.v, SD.get(('POSTV', ci - 1), 0))
                        k.Wv('s', k.x1s, SD.get(('POSTQ', ci - 1), 0))
                        k.Wv('s', k.dbgS, k.dbgS.v)
                    k.Wv('s', k.t, sch['t'])
                    chsb(sch, *[(lambda e, i=i, h=h: e.activation(
                        out=otm[:, h * 64:(h + 1) * 64],
                        in_=psolA[:, i * 64:(i + 1) * 64],
                        func=AF.Identity, scale=Lam8_[:, h:h + 1]))
                        for i, h in enumerate(hs)])
                    k.Wv('v', k.t, sch['t'])
                    k.Wv('v', k.s, sch['s'])
                    vb2 = []
                    for i, h in enumerate(hs):
                        Ss_h = Ssb[:, h * DV:(h + 1) * DV]
                        vb2.append(lambda e, i=i, h=h: e.tensor_add(
                            out=otm[:, h * 64:(h + 1) * 64],
                            in0=otm[:, h * 64:(h + 1) * 64],
                            in1=psolC[:, i * 64:(i + 1) * 64]))
                        vb2.append(lambda e, i=i, h=h, Ss_h=Ss_h: e.scalar_tensor_tensor(
                            out=Ss_h, in0=Ss_h, scalar=gamC_[0:64, h:h + 1],
                            in1=pmisc[0:64, i * 64:(i + 1) * 64],
                            op0=ALU.mult, op1=ALU.add))
                    chvb(sch, *vb2)
                    k.Wv('s', k.v, sch['v'])
                    chsb(sch, *[(lambda e, h=h: e.activation(
                        out=Sbf[0:64, h * DV:(h + 1) * DV],
                        in_=Ssb[:, h * DV:(h + 1) * DV], func=AF.Identity)) for h in hs])
                    odd = [h for h in hs if (h // 2) % 2 == 1]
                    k.Wv('t', k.v, sch['v'])
                    k.Wv('t', k.s, sch['s'])
                    chtb(sch, *[(lambda e, i=i, h=h: e.matmul(
                        pmisc[0:128, i * 64:(i + 1) * 64], lhsT=ishB[:],
                        rhs=Sbf[:, h * DV:(h + 1) * DV], start=True, stop=True))
                        for i, h in enumerate(odd)])
                    k.Wv('v', k.t, sch['t'])
                    chvb(sch, *[(lambda e, i=i, h=h: e.tensor_copy(
                        out=Sb2[:, ((h // 4) * 2 + (h % 2)) * DV:
                                ((h // 4) * 2 + (h % 2) + 1) * DV],
                        in_=pmisc[0:128, i * 64:(i + 1) * 64]))
                        for i, h in enumerate(odd)])
                    SD[('wS', ci, wv2)] = sch['v']
                    SD[('wSs', ci, wv2)] = sch['s']
                    if wv2 == 1:
                        SD[('SEQV', ci)] = sch['v']
                        SD[('SEQT', ci)] = sch['t']
                return wave

            def s4():
                chvb(sch, lambda e: e.tensor_mul(out=otm[:], in0=otm[:],
                                                 in1=zs[:, tt * 1024:(tt + 1) * 1024]))
                k.Wv('s', k.v, max(sch['v'], SD.get(('POSTV', ci - 1), 0)))
                chsb(sch, lambda e: e.activation(out=pscr[:], in_=otm[:], func=AF.Square))
                k.Wv('v', k.s, sch['s'])
                chvb(sch, lambda e: e.tensor_reduce(
                    out=sp_t[:], in_=pscr[:].rearrange("p (h d) -> p h d", h=16),
                    axis=AX.X, op=ALU.add))
                k.Wv('s', k.v, sch['v'])
                chsb(sch, lambda e: e.activation(out=sp_t2[:], in_=sp_t[:], func=AF.Ln,
                                                 scale=1.0 / 64.0, bias=eps6c[:, 0:1]), DR,
                     lambda e: e.activation(out=sp_t2[:], in_=sp_t2[:], func=AF.Exp,
                                            scale=-0.5))
                k.Wv('v', k.s, sch['s'])
                chvb(sch, DR, lambda e: e.tensor_mul(
                    out=otm[:].rearrange("p (h d) -> p h d", h=16),
                    in0=otm[:].rearrange("p (h d) -> p h d", h=16),
                    in1=sp_t2[:].unsqueeze(2).broadcast_to([128, 16, 64])), DR)
                chvb(sch, lambda e: e.tensor_mul(
                    out=xbst[:, tt * 1024:(tt + 1) * 1024].rearrange(
                        "p (h d) -> p h d", h=16),
                    in0=otm[:].rearrange("p (h d) -> p h d", h=16),
                    in1=nw_rep[:].unsqueeze(1).broadcast_to([128, 16, 64])))

            def s5():
                if dbg and g == 0:
                    k.Wv('q', k.dbgS, k.dbgS.v)
                    k.Wv('q', k.v, sch['v'])
                    k.dma(dbg_t["dbg_o"][tt * 128:(tt + 1) * 128, :], otm[:], sem=k.dbgS)
                    k.Wv('v', k.dbgS, k.dbgS.v)
                k.Wv('q', k.hnf, k.hnf.v)
                k.Wv('q', k.v, sch['v'])
                k.Wv('q', k.t, SD.get(('POSTT', ci - 1), 0))
                for ic in range(8):
                    k.dma(hnfm[:, ic * 128:(ic + 1) * 128],
                          xbst[:, tt * 1024 + ic * 128: tt * 1024 + (ic + 1) * 128],
                          transpose=True, sem=k.hnf)
                for oc2 in range(2):
                    ps = psolA[:] if oc2 == 0 else psolB[:]
                    k.Wv('t', k.hnf, k.hnf.v)
                    k.Wv('t', k.v, sch['v'])
                    def mwo(e, oc2=oc2, ps=ps):
                        for vc in range(8):
                            r = e.matmul(ps, lhsT=hnfm[:, vc * 128:(vc + 1) * 128],
                                         rhs=woutT[:, vc * D + oc2 * 512:
                                                  vc * D + (oc2 + 1) * 512],
                                         start=(vc == 0), stop=(vc == 7))
                        return r
                    chtb(sch, mwo)
                    k.Wv('v', k.t, sch['t'])
                    chvb(sch, DR, lambda e, oc2=oc2, ps=ps: e.tensor_add(
                        out=x1t[:, oc2 * 512:(oc2 + 1) * 512], in0=ps,
                        in1=xtm[:, tt * 1024 + oc2 * 512: tt * 1024 + (oc2 + 1) * 512]))
                SD[('POSTT', ci)] = sch['t']
                SD[('POSTV5', ci)] = sch['v']

            def s6():
                k.Wv('v', k.s, max(sch['s'], SD.get(('POSTS', ci - 1), 0)))
                chvb(sch, DR,
                     lambda e: e.scalar_tensor_tensor(out=pscr[:], in0=x1t[:], scalar=1.0,
                                                      in1=x1t[:], op0=ALU.mult,
                                                      op1=ALU.mult), DR,
                     lambda e: e.tensor_reduce(out=cstat[:, 0:1], in_=x1t[:], axis=AX.X,
                                               op=ALU.add),
                     lambda e: e.tensor_reduce(out=cstat[:, 1:2], in_=pscr[:],
                                               axis=AX.X, op=ALU.add), DR,
                     lambda e: e.tensor_scalar(out=cstat[:, 2:3], in0=cstat[:, 0:1],
                                               scalar1=1.0 / D, scalar2=None, op0=ALU.mult),
                     lambda e: e.tensor_scalar(out=cstat[:, 4:5], in0=cstat[:, 1:2],
                                               scalar1=1.0 / D, scalar2=None,
                                               op0=ALU.mult), DR,
                     lambda e: e.tensor_mul(out=cstat[:, 3:4], in0=cstat[:, 2:3],
                                            in1=cstat[:, 2:3]), DR,
                     lambda e: e.tensor_sub(out=cstat[:, 5:6], in0=cstat[:, 4:5],
                                            in1=cstat[:, 3:4]))
                k.Wv('s', k.v, sch['v'])
                chsb(sch, lambda e: e.activation(out=cstat[:, 6:7], in_=cstat[:, 5:6],
                                                 func=AF.Ln, bias=eps5c[:, 0:1]), DR,
                     lambda e: e.activation(out=cstat[:, 6:7], in_=cstat[:, 6:7],
                                            func=AF.Exp, scale=-0.5))
                k.Wv('v', k.s, sch['s'])
                chvb(sch, lambda e: e.tensor_scalar(out=x1t[:], in0=x1t[:],
                                                    scalar1=cstat[:, 2:3],
                                                    scalar2=cstat[:, 6:7],
                                                    op0=ALU.subtract, op1=ALU.mult))
                if dbg and g == 0:
                    k.Wv('q', k.dbgS, k.dbgS.v)
                    k.Wv('q', k.v, sch['v'])
                    k.dma(dbg_t["dbg_x1"][tt * 128:(tt + 1) * 128, :], x1t[:], sem=k.dbgS)
                    k.Wv('v', k.dbgS, k.dbgS.v)
                k.Wv('q', k.x1s, k.x1s.v)
                k.Wv('q', k.v, sch['v'])
                k.dma(x1_d[t0 + tt * 128: t0 + (tt + 1) * 128, :], x1t[:], sem=k.x1s)
                if phases == 'A':
                    k.dma(out_d[t0 + tt * 128: t0 + (tt + 1) * 128, :], x1t[:], sem=k.x1s)
                SD[('POSTV', ci)] = sch['v']
                SD[('POSTS', ci)] = sch['s']
                SD[('POSTQ', ci)] = k.x1s.v

            return [mk_wave(wv2) for wv2 in range(2)] + [s4, s5, s6]

        pres = {tt: pre_stages(g, tt, t0, grp) for tt in range(CPG)}
        seqs = {tt: seqpost_stages(g, tt, t0, grp) for tt in range(CPG)}
        for st in pres[0]:
            st()
        def prefetch_x(tt):
            nt0 = (g + 1) * GT
            pv = SD[('POSTV', g * CPG + tt)]
            k.Wv('q', k.v, pv)
            k.dma(xtm[:, tt * 1024:(tt + 1) * 1024],
                  x_d[nt0 + tt * 128: nt0 + (tt + 1) * 128, :])
            k.Wv('g', k.hnf, k.hnf.v)
            k.Wv('g', k.v, pv)
            k.gdma(xbst[:, tt * 1024:(tt + 1) * 1024],
                   x_d[nt0 + tt * 128: nt0 + (tt + 1) * 128, :])
            k.Wv('q', k.g, k.g.v)
            k.Wv('q', k.t, grp['t'])
            for ic in range(8):
                k.dma(xfm[:, ic * GT + tt * 128: ic * GT + (tt + 1) * 128],
                      xbst[:, tt * 1024 + ic * 128: tt * 1024 + (ic + 1) * 128],
                      transpose=True)

        for tt in range(CPG):
            a = pres[tt + 1] if tt + 1 < CPG else []
            b = seqs[tt]
            ia = ib = 0
            na, nb = len(a), len(b)
            while ia < na or ib < nb:
                if ib < nb and (ia >= na or ib * max(na, 1) <= ia * nb):
                    b[ib]()
                    ib += 1
                else:
                    a[ia]()
                    ia += 1
            if phases == 'full' and g + 1 < NGR:
                prefetch_x(tt)

    esA.close()
    k.barrier()
    if phases in ('A', 'C1', 'C0', 'C0b', 'C0c', 'C0c2', 'C0d', 'C0e'):
        k.W('q', k.q)

    if phases != 'full':
        with es:
            with nc.Block() as blk:
                @blk.sync
                def _(e):
                    for eng, fn in k.ops:
                        if eng == 'q':
                            fn(e)

                @blk.gpsimd
                def _(e):
                    for eng, fn in k.ops:
                        if eng == 'g':
                            fn(e)

                @blk.vector
                def _(e):
                    for eng, fn in k.ops:
                        if eng == 'v':
                            fn(e)

                @blk.scalar
                def _(e):
                    for eng, fn in k.ops:
                        if eng == 's':
                            fn(e)

                @blk.tensor
                def _(e):
                    for eng, fn in k.ops:
                        if eng == 't':
                            fn(e)
        return nc

    esB = ExitStack()
    sB = lambda name, shape, dt: esB.enter_context(nc.sbuf_tensor(name, shape, dt))
    wgT = sB("wgT", [128, 8 * INNER], BF16)
    wuT = sB("wuT", [128, 8 * INNER], BF16)
    wdk = sB("wdk", [128, 3 * 2048], BF16)
    tfh = sB("tfh", [128, 1024], F32)
    stg2f = tfh[:, 0:1024]
    x1g = sB("x1g", [128, 2 * 1024], F32)
    xq = sB("xq", [128, 1024], BF16)
    xqfm = sB("xqfm", [128, 2 * 1024], BF16)
    h2 = sB("h2", [128, 4096], F32)
    hqfm = sB("hqfm", [128, 2 * 4096], BF16)
    sgs = sB("sgs", [128, 8], F32)
    srep = sB("srep", [128, 8], F32)
    scl = sB("scl", [128, 24], F32)
    ot2 = sB("ot2", [128, 1024], F32)
    stgd = h2[:].bitcast(BF16)[:, 0:4096]
    q32 = h2[:].bitcast(I32)
    tscr = sB("tscr", [128, 4096], BF16)
    hq = tscr

    k.vb(lambda e: e.memset(sgs[:], 0.0))

    hqf32 = hqfm[:].bitcast(F32)[:, 0:4096]

    def absmean_w(w_d, O, acc_col):
        wv = w_d if w_d.shape[1] == 1024 else w_d.rearrange("a (n b) -> (a n) b", b=1024)
        for r in range(O // 512):
            pp = r % 2
            stgf = h2[:].rearrange("p (a b) -> p a b", a=4) if pp == 0 \
                else hqf32.rearrange("p (a b) -> p a b", a=4)
            stgff = h2[:] if pp == 0 else hqf32
            if pp == 0:
                k.W('q', k.s)
                k.dma(stgf, wv[r * 512:(r + 1) * 512, :]
                      .rearrange("(a p) b -> p a b", p=128))
                k.W('s', k.q)
            else:
                k.W('g', k.s)
                k.gdma(stgf, wv[r * 512:(r + 1) * 512, :]
                       .rearrange("(a p) b -> p a b", p=128))
                k.W('s', k.g)
            k.W('s', k.v)
            k.sb(lambda e, stgff=stgff, pp=pp: e.activation(
                out=tscr[:], in_=stgff, func=AF.Abs, accum_out=sgs[:, 3 + pp:4 + pp]))
            k.W('v', k.s)
            k.vb(lambda e, acc_col=acc_col, pp=pp: e.tensor_add(
                out=sgs[:, acc_col:acc_col + 1], in0=sgs[:, acc_col:acc_col + 1],
                in1=sgs[:, 3 + pp:4 + pp]))

    absmean_w(wg_d, INNER, 0)
    absmean_w(wu_d, INNER, 1)
    absmean_w(wd_d, D * 4, 2)

    k.W('t', k.v)
    k.tb(lambda e: e.matmul(pmisc[0:1, 0:3], lhsT=onecol[:], rhs=sgs[:, 0:3],
                            start=True, stop=True))
    k.W('v', k.t)
    k.vb(lambda e: e.tensor_copy(out=sgs[0:1, 4:7], in_=pmisc[0:1, 0:3]))
    k.W('q', k.v)
    k.dma(bc2_d[0:1, 32:35], sgs[0:1, 4:7])
    k.W('q', k.q)
    k.dma(srep[:, 0:3], bc2_d[0:1, 32:35].partition_broadcast(128).squeeze(1))
    k.W('v', k.q)
    k.vb(lambda e: e.tensor_scalar(out=srep[:, 0:3], in0=srep[:, 0:3],
                                   scalar1=1.0 / (INNER * D), scalar2=1e-5,
                                   op0=ALU.mult, op1=ALU.max), DR,
         lambda e: e.reciprocal(out=srep[:, 3:6], in_=srep[:, 0:3]))
    pbbf2 = pbig[:].bitcast(BF16)

    def tern_w(w_d, destT, sc):
        for r4 in range(INNER // 512):
            pp = r4 % 2
            stgf = h2[:].rearrange("p (a b) -> p a b", a=4) if pp == 0 \
                else hqf32.rearrange("p (a b) -> p a b", a=4)
            stgff = h2[:] if pp == 0 else hqf32
            if pp == 0:
                k.W('q', k.v)
                k.dma(stgf, w_d[r4 * 512:(r4 + 1) * 512, :]
                      .rearrange("(a p) b -> p a b", p=128))
                k.W('v', k.q)
            else:
                k.W('g', k.v)
                k.gdma(stgf, w_d[r4 * 512:(r4 + 1) * 512, :]
                       .rearrange("(a p) b -> p a b", p=128))
                k.W('v', k.g)
            k.W('v', k.t)
            k.vb(lambda e, sc=sc, stgff=stgff: e.tensor_scalar(
                out=tscr[:], in0=stgff, scalar1=srep[:, sc:sc + 1], scalar2=0.5,
                op0=ALU.mult, op1=ALU.is_gt),
                 lambda e, sc=sc, stgff=stgff: e.tensor_scalar(
                out=wdk[:, 0:4096], in0=stgff, scalar1=srep[:, sc:sc + 1], scalar2=-0.5,
                op0=ALU.mult, op1=ALU.is_lt), DR,
                 lambda e: e.tensor_sub(out=tscr[:], in0=tscr[:], in1=wdk[:, 0:4096]))
            k.W('t', k.v)
            k.W('t', k.s)
            def ttw(e):
                for s_ in range(32):
                    r_ = e.transpose(pbbf2[:, s_ * 128:(s_ + 1) * 128],
                                     in_=tscr[:, s_ * 128:(s_ + 1) * 128],
                                     identity=identB[:])
                return r_
            k.tb(ttw)
            k.W('v', k.t)
            k.vb(*[(lambda e, a=a, ic=ic, r4=r4: e.tensor_copy(
                out=destT[:, ic * INNER + (r4 * 4 + a) * 128:
                         ic * INNER + (r4 * 4 + a) * 128 + 128],
                in_=pbbf2[:, (a * 8 + ic) * 128:(a * 8 + ic + 1) * 128]))
                for a in range(4) for ic in range(8)])

    tern_w(wg_d, wgT, 3)
    tern_w(wu_d, wuT, 4)
    for r in range(D // 128):
        k.W('q', k.v)
        k.W('q', k.s)
        k.dma(h2[:], wd_d[r * 128:(r + 1) * 128, :])
        k.W('v', k.q)
        k.W('v', k.t)
        k.vb(lambda e: e.tensor_scalar(out=tscr[:], in0=h2[:], scalar1=srep[:, 5:6],
                                       scalar2=0.5, op0=ALU.mult, op1=ALU.is_gt),
             lambda e: e.tensor_scalar(out=hqfm[:, 4096:8192], in0=h2[:], scalar1=srep[:, 5:6],
                                       scalar2=-0.5, op0=ALU.mult, op1=ALU.is_lt), DR,
             lambda e: e.tensor_sub(out=tscr[:], in0=tscr[:], in1=hqfm[:, 4096:8192]))
        k.W('t', k.v)
        k.W('t', k.s)
        def ttd(e):
            for kk in range(32):
                r_ = e.transpose(pbbf2[:, kk * 128:(kk + 1) * 128],
                                 in_=tscr[:, kk * 128:(kk + 1) * 128], identity=identB[:])
            return r_
        k.tb(ttd)
        k.W('v', k.t)
        k.W('v', k.q)
        k.W('s', k.t)
        k.W('s', k.q)
        k.vb(*[(lambda e, kk=kk: e.tensor_copy(
            out=hqfm[:, kk * 128:(kk + 1) * 128],
            in_=pbbf2[:, kk * 128:(kk + 1) * 128])) for kk in range(32)])
        k.W('q', k.v)
        k.W('q', k.s)
        k.dma(wdt_d[:, r * 128:(r + 1) * 128].rearrange("(a p) b -> p a b", p=128),
              hqfm[:, 0:4096].rearrange("p (a b) -> p a b", a=32))

    k.W('t', k.v)
    k.W('t', k.s)

    NTT = NGR * 4
    snap_tr = {}
    snap_txq = {}
    snap_txqT = {}
    snap_hqe = {}
    snap_trT = {}
    snap_gu = {}
    snap_dn = {}
    snap_post = {}
    snap_stt = {}
    snap_hq = {}
    wdv = {}
    for ti in range(NTT + 1):
        g, tt = divmod(ti, 4)
        if ti >= 1:
            if ti == 1:
                k.W('g', k.q)
            k.Wv('g', k.t, snap_dn.get(ti - 2, 0))
            for pg in range(3):
                k.dma(wdk[:, pg * 2048:(pg + 1) * 2048].rearrange("p (a b) -> p a b", a=2),
                      wdt_d[pg * 256:(pg + 1) * 256, :].rearrange("(a p) b -> p a b", p=128),
                      sem=k.wd[pg], eng='g')
                wdv[pg] = k.wd[pg].v
        if ti < NTT:
            k.W('q', k.v)
            k.dma(x1g[:, (ti % 2) * 1024:(ti % 2) * 1024 + 1024],
                  x1_d[ti * 128:(ti + 1) * 128, :])
            scb = 12 * (ti % 2)
            xsl = x1g[:, (ti % 2) * 1024:(ti % 2) * 1024 + 1024]
            tq1 = hqfm[:].bitcast(I32)[:, (ti % 2) * 2048:(ti % 2) * 2048 + 2048]
            k.W('v', k.q)
            k.Wv('v', k.t, snap_txqT.get(ti - 1, 0))
            if ti >= 2:
                k.Wv('v', k.v, snap_post.get(ti - 2, 0))
            k.vb(DR, lambda e, xsl=xsl, scb=scb: e.tensor_reduce(
                     out=scl[:, scb:scb + 1], in_=xsl, axis=AX.X,
                     op=ALU.max, apply_absolute_value=True), DR,
                 lambda e, scb=scb: e.tensor_scalar(out=scl[:, scb:scb + 1],
                     in0=scl[:, scb:scb + 1], scalar1=1e-5, scalar2=None, op0=ALU.max), DR,
                 lambda e, scb=scb: e.reciprocal(out=scl[:, scb + 1:scb + 2],
                     in_=scl[:, scb:scb + 1]), DR,
                 lambda e, scb=scb: e.tensor_scalar(out=scl[:, scb + 1:scb + 2],
                     in0=scl[:, scb + 1:scb + 2], scalar1=127.0, scalar2=None, op0=ALU.mult),
                 lambda e, scb=scb: e.tensor_scalar(out=scl[:, scb + 2:scb + 3],
                     in0=scl[:, scb:scb + 1], scalar1=1.0 / 127.0, scalar2=None, op0=ALU.mult), DR,
                 lambda e, scb=scb: e.tensor_mul(out=scl[:, scb + 3:scb + 4],
                     in0=scl[:, scb + 2:scb + 3], in1=srep[:, 0:1]),
                 lambda e, scb=scb: e.tensor_mul(out=scl[:, scb + 4:scb + 5],
                     in0=scl[:, scb + 2:scb + 3], in1=srep[:, 1:2]), DR,
                 lambda e, xsl=xsl, tq1=tq1, scb=scb: e.tensor_scalar(
                     out=tq1[:, 0:1024], in0=xsl, scalar1=scl[:, scb + 1:scb + 2],
                     scalar2=512.5, op0=ALU.mult, op1=ALU.add), DR,
                 lambda e, xsl=xsl, tq1=tq1, scb=scb: e.scalar_tensor_tensor(
                     out=tfh[:, 0:1024], in0=xsl, scalar=scl[:, scb + 1:scb + 2],
                     in1=tq1[:, 0:1024], op0=ALU.mult, op1=ALU.subtract), DR,
                 lambda e: e.tensor_scalar(out=tfh[:, 0:1024], in0=tfh[:, 0:1024],
                     scalar1=-512.5, scalar2=-1.0, op0=ALU.is_lt, op1=ALU.mult), DR,
                 lambda e, tq1=tq1: e.scalar_tensor_tensor(
                     out=xq[:], in0=tfh[:, 0:1024], scalar=-512.0, in1=tq1[:, 0:1024],
                     op0=ALU.add, op1=ALU.add))

        if ti >= 1:
            pi = ti - 1
            pscb = 12 * (pi % 2)
            k.Wv('t', k.v, snap_tr[pi])
            if pi >= 1:
                k.Wv('t', k.v, snap_post.get(pi - 1, 0))
            for grp in range(16):
                par = (grp % 3) * 2048
                k.Wv('t', k.wd[grp % 3], wdv[grp])
                def mdn(e, grp=grp, par=par, pi=pi):
                    for a in range(2):
                        kk = grp * 2 + a
                        for oc in range(2):
                            r = e.matmul(psolA[:] if oc == 0 else psolB[:],
                                         lhsT=hqfm[:, (pi % 2) * 4096 + kk * 128:
                                                   (pi % 2) * 4096 + (kk + 1) * 128],
                                         rhs=wdk[:, par + a * 1024 + oc * 512:
                                                 par + a * 1024 + (oc + 1) * 512],
                                         start=(kk == 0), stop=(kk == 31))
                    return r
                k.tb(mdn)
                if grp + 3 < 16:
                    k.W('g', k.t)
                    k.dma(wdk[:, par:par + 2048].rearrange("p (a b) -> p a b", a=2),
                          wdt_d[(grp + 3) * 256:(grp + 4) * 256, :]
                          .rearrange("(a p) b -> p a b", p=128),
                          sem=k.wd[grp % 3], eng='g')
                    wdv[grp + 3] = k.wd[grp % 3].v
            snap_down = k.t.v
            snap_dn[pi] = k.t.v
        if ti < NTT:
            k.W('t', k.v)
            def txq(e):
                for ic in range(8):
                    r = e.transpose(psolC[:].bitcast(BF16)[:, ic * 128:(ic + 1) * 128],
                                    in_=xq[:, ic * 128:(ic + 1) * 128], identity=identB[:])
                return r
            k.tb(txq)
            snap_txqT[ti] = k.t.v
            k.W('v', k.t)
            k.vb(*[(lambda e, ic=ic, ti=ti: e.tensor_copy(
                out=xqfm[:, (ti % 2) * 1024 + ic * 128:(ti % 2) * 1024 + (ic + 1) * 128],
                in_=psolC[:].bitcast(BF16)[:, ic * 128:(ic + 1) * 128])) for ic in range(8)])
            snap_txq[ti] = k.v.v
            k.Wv('t', k.v, snap_txq[ti])
            k.W('s', k.o)
            for cc in range(8):
                psg = pbig[:, ((2 * cc) % 4) * 512:(((2 * cc) % 4) + 1) * 512]
                psu = pbig[:, ((2 * cc + 1) % 4) * 512:(((2 * cc + 1) % 4) + 1) * 512]
                if cc >= 2:
                    k.Wv('t', k.v, snap_stt.get((ti, cc - 2), 0))
                elif ti >= 1:
                    k.Wv('t', k.v, max(snap_stt.get((ti - 1, 6 + cc), 0),
                                       snap_hqe.get(ti - 1, 0)))
                def mgu(e, cc=cc, psg=psg, psu=psu, ti=ti):
                    xb = (ti % 2) * 1024
                    for ic in range(8):
                        e.matmul(psg, lhsT=xqfm[:, xb + ic * 128: xb + (ic + 1) * 128],
                                 rhs=wgT[:, ic * INNER + cc * 512: ic * INNER + (cc + 1) * 512],
                                 start=(ic == 0), stop=(ic == 7))
                    for ic in range(8):
                        r = e.matmul(psu, lhsT=xqfm[:, xb + ic * 128: xb + (ic + 1) * 128],
                                     rhs=wuT[:, ic * INNER + cc * 512: ic * INNER + (cc + 1) * 512],
                                     start=(ic == 0), stop=(ic == 7))
                    return r
                k.tb(mgu)
                k.W('s', k.t)
                k.W('s', k.v)
                k.sb(lambda e, cc=cc, psg=psg, scb=scb: e.activation(
                    out=ot2[:, (cc % 2) * 512:(cc % 2) * 512 + 512], in_=psg, func=AF.Silu,
                    scale=scl[:, scb + 3:scb + 4]))
                k.W('v', k.s)
                k.vb(lambda e, cc=cc, psu=psu, scb=scb: e.scalar_tensor_tensor(
                    out=h2[:, cc * 512:(cc + 1) * 512], in0=psu, scalar=scl[:, scb + 4:scb + 5],
                    in1=ot2[:, (cc % 2) * 512:(cc % 2) * 512 + 512], op0=ALU.mult, op1=ALU.mult))
                snap_stt[(ti, cc)] = k.v.v
            snap_gu[ti] = k.t.v
            if ti >= 1:
                k.Wv('v', k.t, snap_trT[ti - 1])
            k.vb(DR, lambda e, scb=scb: e.tensor_reduce(out=scl[:, scb + 5:scb + 6], in_=h2[:],
                     axis=AX.X, op=ALU.max, apply_absolute_value=True), DR,
                 lambda e, scb=scb: e.tensor_scalar(out=scl[:, scb + 5:scb + 6],
                     in0=scl[:, scb + 5:scb + 6], scalar1=1e-5, scalar2=None, op0=ALU.max), DR,
                 lambda e, scb=scb: e.reciprocal(out=scl[:, scb + 6:scb + 7],
                     in_=scl[:, scb + 5:scb + 6]), DR,
                 lambda e, scb=scb: e.tensor_scalar(out=scl[:, scb + 6:scb + 7],
                     in0=scl[:, scb + 6:scb + 7], scalar1=127.0, scalar2=None, op0=ALU.mult))
            snap_q2 = {}
            for w2 in range(4):
                hsl = h2[:, w2 * 1024:(w2 + 1) * 1024]
                k.vb(DR, lambda e, hsl=hsl, tq1=tq1, scb=scb: e.tensor_scalar(
                         out=tq1[:, 0:1024], in0=hsl, scalar1=scl[:, scb + 6:scb + 7],
                         scalar2=512.5, op0=ALU.mult, op1=ALU.add), DR,
                     lambda e, hsl=hsl, tq1=tq1, scb=scb: e.scalar_tensor_tensor(
                         out=tfh[:, 0:1024], in0=hsl, scalar=scl[:, scb + 6:scb + 7],
                         in1=tq1[:, 0:1024], op0=ALU.mult, op1=ALU.subtract), DR,
                     lambda e: e.tensor_scalar(out=tfh[:, 0:1024], in0=tfh[:, 0:1024],
                         scalar1=-512.5, scalar2=-1.0, op0=ALU.is_lt, op1=ALU.mult), DR,
                     lambda e, w2=w2, tq1=tq1: e.scalar_tensor_tensor(
                         out=hq[:, w2 * 1024:(w2 + 1) * 1024], in0=tfh[:, 0:1024],
                         scalar=-512.0, in1=tq1[:, 0:1024], op0=ALU.add, op1=ALU.add))
                if w2 % 2 == 1:
                    snap_q2[w2 // 2] = k.v.v
            k.vb(DR, lambda e, scb=scb: e.scalar_tensor_tensor(
                out=scl[:, scb + 7:scb + 8], in0=scl[:, scb + 5:scb + 6],
                scalar=1.0 / 127.0, in1=srep[:, 2:3], op0=ALU.mult, op1=ALU.mult))
            for hhalf in range(2):
                k.Wv('t', k.v, snap_q2[hhalf])
                def tq(e, hhalf=hhalf):
                    for kk in range(hhalf * 16, hhalf * 16 + 16):
                        r = e.transpose(pbbf2[:, kk * 128:(kk + 1) * 128],
                                        in_=hq[:, kk * 128:(kk + 1) * 128],
                                        identity=identB[:])
                    return r
                k.tb(tq)
                k.W('v', k.t)
                k.vb(*[(lambda e, kk=kk, ti=ti: e.tensor_copy(
                    out=hqfm[:, (ti % 2) * 4096 + kk * 128:
                             (ti % 2) * 4096 + (kk + 1) * 128],
                    in_=pbbf2[:, kk * 128:(kk + 1) * 128]))
                    for kk in range(hhalf * 16, hhalf * 16 + 16)])
            snap_trT[ti] = k.t.v
            snap_tr[ti] = k.v.v
            snap_hqe[ti] = k.v.v
        if ti >= 1:
            pi = ti - 1
            pg, ptt = divmod(pi, 4)
            pscb = 12 * (pi % 2)
            k.Wv('v', k.t, snap_down)
            k.W('v', k.o)
            pxb = (pi % 2) * 1024
            k.vb(DR, lambda e, pxb=pxb, pscb=pscb: e.scalar_tensor_tensor(
                     out=ot2[:, 0:512], in0=psolA[:], scalar=scl[:, pscb + 7:pscb + 8],
                     in1=x1g[:, pxb: pxb + 512], op0=ALU.mult, op1=ALU.add),
                 lambda e, pxb=pxb, pscb=pscb: e.scalar_tensor_tensor(
                     out=ot2[:, 512:1024], in0=psolB[:], scalar=scl[:, pscb + 7:pscb + 8],
                     in1=x1g[:, pxb + 512: pxb + 1024],
                     op0=ALU.mult, op1=ALU.add), DR,
                 lambda e, pscb=pscb: e.scalar_tensor_tensor(out=tfh[:, 0:1024]
                     if False else h2[:, 0:1024], in0=ot2[:], scalar=1.0,
                     in1=ot2[:], op0=ALU.mult, op1=ALU.mult), DR,
                 lambda e, pscb=pscb: e.tensor_reduce(out=scl[:, pscb + 8:pscb + 9],
                     in_=ot2[:], axis=AX.X, op=ALU.add),
                 lambda e, pscb=pscb: e.tensor_reduce(out=scl[:, pscb + 9:pscb + 10],
                     in_=h2[:, 0:1024], axis=AX.X, op=ALU.add), DR,
                 lambda e, pscb=pscb: e.tensor_scalar(out=scl[:, pscb + 8:pscb + 9],
                     in0=scl[:, pscb + 8:pscb + 9], scalar1=1.0 / D, scalar2=None, op0=ALU.mult),
                 lambda e, pscb=pscb: e.tensor_scalar(out=scl[:, pscb + 9:pscb + 10],
                     in0=scl[:, pscb + 9:pscb + 10], scalar1=1.0 / D, scalar2=None,
                     op0=ALU.mult), DR,
                 lambda e, pscb=pscb: e.tensor_mul(out=scl[:, pscb + 10:pscb + 11],
                     in0=scl[:, pscb + 8:pscb + 9], in1=scl[:, pscb + 8:pscb + 9]), DR,
                 lambda e, pscb=pscb: e.tensor_sub(out=scl[:, pscb + 9:pscb + 10],
                     in0=scl[:, pscb + 9:pscb + 10], in1=scl[:, pscb + 10:pscb + 11]))
            k.W('s', k.v)
            k.sb(lambda e, pscb=pscb: e.activation(out=scl[:, pscb + 11:pscb + 12],
                     in_=scl[:, pscb + 9:pscb + 10], func=AF.Ln, bias=eps5c[:, 0:1]), DR,
                 lambda e, pscb=pscb: e.activation(out=scl[:, pscb + 11:pscb + 12],
                     in_=scl[:, pscb + 11:pscb + 12], func=AF.Exp, scale=-0.5))
            k.W('v', k.s)
            k.vb(lambda e, pscb=pscb: e.tensor_scalar(out=ot2[:], in0=ot2[:],
                     scalar1=scl[:, pscb + 8:pscb + 9], scalar2=scl[:, pscb + 11:pscb + 12],
                     op0=ALU.subtract, op1=ALU.mult))
            snap_post[pi] = k.v.v
            k.W('q', k.v)
            k.dma(out_d[pg * GT + ptt * 128: pg * GT + (ptt + 1) * 128, :], ot2[:], sem=k.o)

    esB.close()
    k.W('q', k.q)
    k.W('q', k.o)

    with es:
        with nc.Block() as blk:
            @blk.sync
            def _(e):
                for eng, fn in k.ops:
                    if eng == 'q':
                        fn(e)

            @blk.gpsimd
            def _(e):
                for eng, fn in k.ops:
                    if eng == 'g':
                        fn(e)

            @blk.vector
            def _(e):
                for eng, fn in k.ops:
                    if eng == 'v':
                        fn(e)

            @blk.scalar
            def _(e):
                for eng, fn in k.ops:
                    if eng == 's':
                        fn(e)

            @blk.tensor
            def _(e):
                for eng, fn in k.ops:
                    if eng == 't':
                        fn(e)
    return nc


def shard_inputs(inputs):
    x = np.ascontiguousarray(np.asarray(inputs["L0_attn_hidden_states"], np.float32))
    wba = np.ascontiguousarray(np.concatenate(
        [np.asarray(inputs["L0_attn_W_b"], np.float32),
         np.asarray(inputs["L0_attn_W_a"], np.float32)], axis=0))
    common = {
        "wqkv": np.ascontiguousarray(np.asarray(inputs["L0_attn_W_qkv"], np.float32)),
        "convw": np.ascontiguousarray(np.asarray(inputs["L0_attn_conv_w"], np.float32)
                                      .reshape(CONV, 4)),
        "wz": np.ascontiguousarray(np.asarray(inputs["L0_attn_W_z"], np.float32)),
        "wba": wba,
        "dtb": np.ascontiguousarray(np.asarray(inputs["L0_attn_dt_bias"], np.float32)
                                    .reshape(1, NH)),
        "alog": np.ascontiguousarray(np.asarray(inputs["L0_attn_A_log"], np.float32)
                                     .reshape(1, NH)),
        "nw": np.ascontiguousarray(np.asarray(inputs["L0_attn_norm_weight"], np.float32)
                                   .reshape(1, DV)),
        "wout": np.ascontiguousarray(np.asarray(inputs["L0_attn_W_out"], np.float32)),
        "wg": np.ascontiguousarray(np.asarray(inputs["L0_ffn_w_gate"], np.float32)),
        "wu": np.ascontiguousarray(np.asarray(inputs["L0_ffn_w_up"], np.float32)),
        "wd": np.ascontiguousarray(np.asarray(inputs["L0_ffn_w_down"], np.float32)),
    }
    return x, common


def kernel(**inputs):
    x, common = shard_inputs(inputs)
    B = x.shape[0]
    nc = build_nc(x.shape[1])
    in_maps = [dict(common, x=np.ascontiguousarray(x[i])) for i in range(B)]
    res = run_bass_kernel_spmd(nc, in_maps, core_ids=list(range(B)))
    return np.stack([r["out"] for r in res.results])
